# version 01
# baseline (speedup 1.0000x reference)
import math
import numpy as np
from contextlib import ExitStack

import concourse.bass as bass
import concourse.mybir as mybir
from concourse.bass_utils import run_bass_kernel_spmd
from concourse.alu_op_type import AluOpType as OP
from concourse.bass_isa import ReduceOp

F32 = mybir.dt.float32
BF16 = mybir.dt.bfloat16
I32 = mybir.dt.int32
AF = mybir.ActivationFunctionType
AX = mybir.AxisListType

NCORES = 8
EXPC = 55.0
TFULL = 16384
TCORE = TFULL // NCORES
DM = 1024

ENGS = ("sync", "tensor", "vector", "scalar", "gpsimd")
DEBUG = False


def bf16():
    import ml_dtypes
    return ml_dtypes.bfloat16


class K:
    def __init__(self):
        self.nc = bass.Bass("TRN2", target_bir_lowering=False)
        self.stack = ExitStack()
        self.queues = {e: [] for e in ENGS}
        self.cnt = {}
        self.hwm = {}
        self.sems = {}

    def sem(self, name):
        if name not in self.sems:
            self.sems[name] = self.stack.enter_context(self.nc.semaphore(name))
            self.cnt[name] = 0
        return self.sems[name]

    def sbuf(self, name, shape, dtype):
        return self.stack.enter_context(self.nc.sbuf_tensor(name, list(shape), dtype))

    def psum(self, name, shape, dtype=F32):
        return self.stack.enter_context(self.nc.psum_tensor(name, list(shape), dtype))

    def dram_in(self, name, shape, dtype):
        return self.nc.dram_tensor(name, list(shape), dtype, kind="ExternalInput").ap()

    def dram_out(self, name, shape, dtype):
        return self.nc.dram_tensor(name, list(shape), dtype, kind="ExternalOutput").ap()

    def dram_tmp(self, name, shape, dtype):
        return self.nc.dram_tensor(name, list(shape), dtype).ap()

    def fence(self, eng, dma=False):
        sname = (f"d_{eng}" if dma else f"c_{eng}")
        return (sname, self.cnt.get(sname, 0))

    AUTO_SERIAL = ("vector", "scalar", "gpsimd")

    def op(self, eng, fn, deps=(), inc=False, dma=False):
        q = self.queues[eng]
        deps = list(deps)
        if eng in self.AUTO_SERIAL:
            deps.append(self.fence(eng))
            if not dma:
                inc = True
        for t in deps:
            if t is None:
                continue
            sname, val = t
            if val <= 0:
                continue
            if eng == "tensor" and sname == "c_tensor":
                continue
            key = (eng, sname)
            if self.hwm.get(key, 0) < val:
                self.sem(sname)
                q.append(("wait", sname, val))
                self.hwm[key] = val
        if not inc:
            q.append(("op", fn, None, 0))
            return None
        sname = f"d_{eng}" if dma else f"c_{eng}"
        self.sem(sname)
        amt = 16 if dma else 1
        q.append(("op", fn, sname, amt))
        self.cnt[sname] += amt
        return (sname, self.cnt[sname])

    def dma(self, eng, out, in_, deps=(), **kw):
        deps = list(deps) + [self.fence(eng, dma=True)]
        return self.op(eng, lambda e: e.dma_start(out=out, in_=in_, **kw),
                       deps=deps, inc=True, dma=True)

    def finalize(self):
        blk = self.stack.enter_context(self.nc.Block())
        for eng in ENGS:
            q = self.queues[eng]
            if not q:
                continue

            def play(e, q=q):
                for item in q:
                    if item[0] == "wait":
                        e.wait_ge(self.sems[item[1]], item[2])
                    else:
                        _, fn, sname, amt = item
                        inst = fn(e)
                        if sname is not None:
                            inst.then_inc(self.sems[sname], amt)

            getattr(blk, eng)(play)


def run_launch(build_fn, in_maps, **kw):
    k = K()
    build_fn(k)
    k.finalize()
    try:
        res = run_bass_kernel_spmd(k.nc, in_maps, core_ids=list(range(NCORES)), **kw)
    finally:
        k.stack.close()
    return res.results


class FM:

    def __init__(self, k, name, P, T, dtype):
        self.P, self.T, self.dtype = P, T, dtype
        self.nt = (P + 127) // 128
        self.sb = k.sbuf(name, [128, self.nt * T], dtype)

    def rows(self, j):
        return min(128, self.P - j * 128)

    def t(self, j, c0=0, c1=None, r0=0, r1=None):
        c1 = self.T if c1 is None else c1
        r1 = self.rows(j) if r1 is None else r1
        return self.sb[r0:r1, j * self.T + c0: j * self.T + c1]

    def view3(self, c0=0, c1=None):
        c1 = self.T if c1 is None else c1
        ap = self.sb[:, 0:self.nt * self.T].rearrange("p (n t) -> p n t", n=self.nt)
        return ap[:, :, c0:c1]


def load_fm(k, fm, hbm_ap, deps=(), eng="sync", dst=(0, None), src=(0, None)):
    d0, d1 = dst[0], (fm.T if dst[1] is None else dst[1])
    s0 = src[0]
    s1 = s0 + (d1 - d0) if src[1] is None else src[1]
    assert s1 - s0 == d1 - d0
    if fm.nt > 1:
        sap = hbm_ap.rearrange("(n p) t -> p n t", p=128)[:, :, s0:s1]
        return k.dma(eng, fm.view3(d0, d1), sap, deps=deps)
    return k.dma(eng, fm.t(0, d0, d1), hbm_ap[:, s0:s1], deps=deps)


def store_fm(k, fm, hbm_ap, deps=(), eng="sync", src=(0, None), dst=(0, None)):
    s0, s1 = src[0], (fm.T if src[1] is None else src[1])
    d0 = dst[0]
    d1 = d0 + (s1 - s0) if dst[1] is None else dst[1]
    assert s1 - s0 == d1 - d0
    if fm.nt > 1:
        dap = hbm_ap.rearrange("(n p) t -> p n t", p=128)[:, :, d0:d1]
        return k.dma(eng, dap, fm.view3(s0, s1), deps=deps)
    return k.dma(eng, hbm_ap[:, d0:d1], fm.t(0, s0, s1), deps=deps)


def cast_fm(k, dst, src, deps=(), eng="vector", c0=0, c1=None, dst_c0=None):
    tok = None
    d0 = c0 if dst_c0 is None else dst_c0
    n = (c1 if c1 is not None else src.T) - c0
    for j in range(src.nt):
        tok = k.op(eng, lambda e, j=j: e.tensor_copy(dst.t(j, d0, d0 + n), src.t(j, c0, c0 + n)),
                   deps=deps if j == 0 else (), inc=(j == src.nt - 1))
    return tok


class PsumPool:
    def __init__(self, k, n, name, shape=(128, 512)):
        self.ts = [k.psum(f"{name}{i}", list(shape)) for i in range(n)]
        self.free = [None] * n
        self.i = 0

    def acquire(self):
        i = self.i
        self.i = (self.i + 1) % len(self.ts)
        return i, self.ts[i], self.free[i]

    def release(self, i, tok):
        self.free[i] = tok


class WeightFM:

    def __init__(self, k, name, D, F):
        self.D, self.F = D, F
        self.nK = (D + 127) // 128
        self.nF = (F + 127) // 128
        self.sb = k.sbuf(name, [128, self.nK * F], BF16)
        self.tok = None

    def lhsT(self, kc, fo):
        f0, f1 = fo * 128, min(self.F, (fo + 1) * 128)
        kr = min(128, self.D - kc * 128)
        return self.sb[0:kr, kc * self.F + f0: kc * self.F + f1]

    def prep(self, k, fo):
        return self.lhsT, [self.tok]

    def done(self, k, fo, mm_tok):
        pass


def load_weight(k, w, hbm_ap, raw_sb, deps=(), eng="sync", cast_eng="scalar",
                chunked=True):
    toks = []
    nK, F = w.nK, w.F
    prev_cast = None
    for kc in range(nK):
        kr = min(128, w.D - kc * 128)
        src = hbm_ap[kc * 128: kc * 128 + kr, :]
        d = list(deps) + ([prev_cast] if prev_cast else [])
        t = k.dma(eng, raw_sb[0:kr, 0:F], src, deps=d)
        prev_cast = k.op(cast_eng, lambda e, kc=kc, kr=kr: e.activation(
            out=w.sb[0:kr, kc * F:(kc + 1) * F], in_=raw_sb[0:kr, 0:F], func=AF.Copy),
            deps=[t], inc=True)
        toks.append(prev_cast)
    w.tok = prev_cast
    return prev_cast


class StagePool:

    def __init__(self, k, name, nK_max, dtype, nbuf=2):
        self.bufs = [k.sbuf(f"{name}{i}", [128, nK_max * 128], dtype)
                     for i in range(nbuf)]
        self.free = [None] * nbuf
        self.i = 0

    def acquire(self):
        i = self.i
        self.i = (self.i + 1) % len(self.bufs)
        return i, self.bufs[i], self.free[i]

    def release(self, i, tok):
        self.free[i] = tok


class StreamW:

    def __init__(self, k, name, hbm_ap, D, F, stage, nbuf=2,
                 eng="sync", cast_eng="gpsimd"):
        self.D, self.F = D, F
        self.nK = (D + 127) // 128
        self.nF = (F + 127) // 128
        self.hbm = hbm_ap
        self.stage = stage
        self.eng, self.cast_eng = eng, cast_eng
        self.bf = [k.sbuf(f"{name}_b{i}", [128, self.nK * 128], BF16)
                   for i in range(nbuf)]
        self.nbuf = nbuf
        self.last_use = [None] * nbuf
        self.src3 = (hbm_ap.rearrange("(n p) f -> p n f", p=128)
                     if self.nK > 1 else None)

    def prep(self, k, fo):
        b = fo % self.nbuf
        f0, f1 = fo * 128, min(self.F, (fo + 1) * 128)
        fw = f1 - f0
        si, raw, sfree = self.stage.acquire()
        if self.nK > 1:
            src = self.src3[:, :, f0:f1]
            dst = raw[:, 0:self.nK * fw].rearrange("p (n f) -> p n f", n=self.nK)
        else:
            src = self.hbm[:, f0:f1]
            dst = raw[0:self.D, 0:fw]
        t = k.dma(self.eng, dst, src, deps=[sfree] if sfree else ())
        tc = k.op(self.cast_eng, lambda e, b=b, fw=fw, raw=raw: e.tensor_copy(
            self.bf[b][:, 0:self.nK * fw], raw[:, 0:self.nK * fw]),
            deps=[t, self.last_use[b]], inc=True)
        self.stage.release(si, tc)

        def lhsT(kc, fo_, b=b, fw=fw):
            kr = min(128, self.D - kc * 128)
            return self.bf[b][0:kr, kc * fw:(kc + 1) * fw]
        return lhsT, [tc]

    def done(self, k, fo, mm_tok):
        self.last_use[fo % self.nbuf] = mm_tok


def matmul_fm(k, pool, *, w, nK, nF, T, rhs, evac, evac_eng="scalar",
              mm_deps=(), M=None, Tc=512):
    ev_tok = None
    for fo in range(nF):
        m = M(fo) if M else 128
        lhsT, extra = w.prep(k, fo)
        last_mm = None
        for c0 in range(0, T, Tc):
            c1 = min(T, c0 + Tc)
            i, ps, ftok = pool.acquire()
            deps = list(mm_deps) + extra + [ftok]
            mm_tok = None
            for kc in range(nK):
                start, stop = kc == 0, kc == nK - 1
                lap = lhsT(kc, fo)
                rap = rhs(kc, c0, c1)
                fn = (lambda e, lap=lap, rap=rap, c0=c0, c1=c1, ps=ps, m=m,
                      start=start, stop=stop:
                      e.matmul(ps[0:m, 0:c1 - c0], lap, rap, start=start, stop=stop))
                mm_tok = k.op("tensor", fn, deps=deps if kc == 0 else (), inc=stop)
            last_mm = mm_tok
            ev_tok = k.op(evac_eng,
                          lambda e, ps=ps, fo=fo, c0=c0, c1=c1, m=m:
                          evac(e, ps[0:m, 0:c1 - c0], fo, c0, c1),
                          deps=[mm_tok], inc=True)
            pool.release(i, ev_tok)
        w.done(k, fo, last_mm)
    return ev_tok


class Consts:
    def __init__(self, k):
        self.h_or = k.dram_in("c_ones_row", [1, 128], F32)
        self.h_orb = k.dram_in("c_ones_row_bf", [1, 128], BF16)
        self.h_ocb = k.dram_in("c_ones_col_bf", [128, 1], BF16)
        self.h_id = k.dram_in("c_ident_bf", [128, 128], BF16)
        self.h_idf = k.dram_in("c_ident_f32", [128, 128], F32)
        self.or_ = k.sbuf("cs_or", [1, 128], F32)
        self.orb = k.sbuf("cs_orb", [1, 128], BF16)
        self.ocb = k.sbuf("cs_ocb", [128, 1], BF16)
        self.id = k.sbuf("cs_id", [128, 128], BF16)
        self.idf = k.sbuf("cs_idf", [128, 128], F32)
        k.dma("sync", self.or_[:], self.h_or)
        k.dma("sync", self.orb[:], self.h_orb)
        k.dma("sync", self.ocb[:], self.h_ocb)
        k.dma("sync", self.idf[:], self.h_idf)
        self.tok = k.dma("sync", self.id[:], self.h_id)


def const_inputs():
    b = bf16()
    return {
        "c_ones_row": np.ones((1, 128), np.float32),
        "c_ones_row_bf": np.ones((1, 128), np.float32).astype(b),
        "c_ones_col_bf": np.ones((128, 1), np.float32).astype(b),
        "c_ident_bf": np.eye(128, dtype=np.float32).astype(b),
        "c_ident_f32": np.eye(128, dtype=np.float32),
    }


def colify(vec, pad=0.0):
    v = np.ascontiguousarray(np.asarray(vec, np.float32))
    n = (v.size + 127) // 128
    if v.size % 128:
        v = np.concatenate([v, np.full(n * 128 - v.size, pad, v.dtype)])
    return np.ascontiguousarray(v.reshape(n, 128).T)


def bcast_chunk(k, pool, cs, dst_ap, row_ap, n_part, deps, n=None):
    i, ps, ftok = pool.acquire()
    if n is None:
        n = row_ap.ap[-1][1]
    mm = k.op("tensor", lambda e: e.matmul(ps[0:n_part, 0:n], cs.or_[0:1, 0:n_part],
                                           row_ap, start=True, stop=True),
              deps=list(deps) + [ftok, cs.tok], inc=True)
    tok = k.op("vector", lambda e: e.tensor_copy(dst_ap, ps[0:n_part, 0:n]),
               deps=[mm], inc=True)
    pool.release(i, tok)
    return tok


def ln_chunk(k, pool, cs, *, r_fm, sq_bf, rb_bf, y_fm, yb_fm, stats_sb, bias_sb,
             r_toks, n, eps=1e-5, gcol=None, bcol=None, guard=()):
    nt, P = r_fm.nt, float(r_fm.P)
    shared = sq_bf is rb_bf
    rb_toks = [k.op("gpsimd", lambda e, j=j: e.tensor_copy(rb_bf.t(j, 0, n), r_fm.t(j, 0, n)),
                    deps=[r_toks[j]] + list(guard), inc=True) for j in range(nt)]
    i1, ps1, f1 = pool.acquire()
    i2, ps2, f2 = pool.acquire()
    mm1 = mm2 = None
    for j in range(nt):
        mm1 = k.op("tensor", lambda e, j=j, ps=ps1: e.matmul(
            ps[0:1, 0:n], cs.ocb[:, 0:1], rb_bf.t(j, 0, n),
            start=(j == 0), stop=(j == nt - 1)),
            deps=([f1, cs.tok] + rb_toks) if j == 0 else (), inc=(j == nt - 1))
    sq_toks = [k.op("scalar", lambda e, j=j: e.activation(
        out=sq_bf.t(j, 0, n), in_=r_fm.t(j, 0, n), func=AF.Square),
        deps=[r_toks[j]] + list(guard) + ([mm1] if shared else []),
        inc=True) for j in range(nt)]
    for j in range(nt):
        mm2 = k.op("tensor", lambda e, j=j, ps=ps2: e.matmul(
            ps[0:1, 0:n], cs.ocb[:, 0:1], sq_bf.t(j, 0, n),
            start=(j == 0), stop=(j == nt - 1)),
            deps=([f2] + sq_toks) if j == 0 else (), inc=(j == nt - 1))
    mu = stats_sb[0:1, 0:n]
    var = stats_sb[0:1, n:2 * n]
    c2r = stats_sb[0:1, 2 * n:3 * n]

    t_mu = k.op("vector", lambda e: e.tensor_scalar_mul(mu, ps1[0:1, 0:n], 1.0 / P),
                deps=[mm1])
    t_v0 = k.op("vector", lambda e: e.tensor_scalar_mul(var, ps2[0:1, 0:n], 1.0 / P),
                deps=[mm2])
    pool.release(i1, t_mu)
    pool.release(i2, t_v0)
    k.op("vector", lambda e: e.tensor_tensor(c2r, mu, mu, op=OP.mult))
    k.op("vector", lambda e: e.tensor_tensor(var, var, c2r, op=OP.subtract))
    t_r = k.op("vector", lambda e: e.tensor_scalar_add(var, var, eps))
    t_ln = k.op("scalar", lambda e: e.activation(out=var, in_=var, func=AF.Ln),
                deps=[t_r], inc=True)
    t_rs = k.op("scalar", lambda e: e.activation(out=var, in_=var, func=AF.Exp, scale=-0.5),
                deps=[t_ln], inc=True)
    k.op("vector", lambda e: e.tensor_tensor(c2r, mu, var, op=OP.mult), deps=[t_rs])
    t_c2 = k.op("vector", lambda e: e.tensor_scalar_mul(c2r, c2r, -1.0))
    b1 = bias_sb[:, 0:n]
    b2 = bias_sb[:, n:2 * n]
    tb1 = bcast_chunk(k, pool, cs, b1, var, 128, [t_rs], n=n)
    tb2 = bcast_chunk(k, pool, cs, b2, c2r, 128, [t_c2], n=n)
    tile_toks = []
    tok = None
    for j in range(nt):
        k.op("vector", lambda e, j=j: e.tensor_tensor(
            y_fm.t(j, 0, n), r_fm.t(j, 0, n), b1, op=OP.mult),
            deps=[tb1, tb2, sq_toks[j], mm1])
        t_y = k.op("vector", lambda e, j=j: e.tensor_tensor(
            y_fm.t(j, 0, n), y_fm.t(j, 0, n), b2, op=OP.add))
        if gcol is not None:
            t_y = k.op("vector", lambda e, j=j: e.tensor_scalar(
                y_fm.t(j, 0, n), y_fm.t(j, 0, n), gcol(j), bcol(j),
                op0=OP.mult, op1=OP.add), deps=[t_y], inc=True)
        if yb_fm is not None:
            t_y = k.op("scalar", lambda e, j=j: e.activation(
                out=yb_fm.t(j, 0, n), in_=y_fm.t(j, 0, n), func=AF.Copy),
                deps=[t_y], inc=True)
        tile_toks.append(t_y)
        tok = t_y
    return tok, tile_toks, (mm1, mm2)



def build_l3f(k):
    T = TCORE
    Tc = 512
    cs = Consts(k)
    pool = PsumPool(k, 4, "pp")
    x_h = k.dram_in("x", [DM, T], F32)
    y_h = k.dram_out("y", [DM, T], F32)
    stage = StagePool(k, "stg", 24, I32)
    wg = StreamW(k, "wg", k.dram_in("wgT", [DM, 3072], I32), DM, 3072, stage)
    wu = StreamW(k, "wu", k.dram_in("wuT", [DM, 3072], I32), DM, 3072, stage)
    wd = StreamW(k, "wd", k.dram_in("wdT", [3072, DM], I32), 3072, DM, stage)
    SC = 1.0 / 127.0

    xb = FM(k, "xb", DM, T, BF16)
    xc = FM(k, "xc", DM, Tc, F32)
    hb = FM(k, "hb", 3072, Tc, BF16)
    s_c = FM(k, "sc", DM, Tc, F32)
    rb = FM(k, "rb", DM, Tc, BF16)
    sq = FM(k, "sq", DM, Tc, BF16)
    stats = k.sbuf("st", [1, 3 * Tc], F32)
    bias2 = k.sbuf("bi", [128, 2 * Tc], F32)

    tx_full = None
    for c0 in range(0, T, Tc):
        t = load_fm(k, xc, x_h, src=(c0, c0 + Tc), deps=[tx_full] if tx_full else ())
        tx_full = None
        for j in range(8):
            tx_full = k.op("vector", lambda e, j=j, c0=c0: e.tensor_copy(
                xb.t(j, c0, c0 + Tc), xc.t(j, 0, Tc)), deps=[t] if j == 0 else (),
                inc=(j == 7))
    txb = tx_full

    dbg_h = k.dram_out("dbg_h", [128, 24 * Tc], BF16) if DEBUG else None
    dbg_g = k.dram_out("dbg_g", [128, 24 * Tc], BF16) if DEBUG else None
    dbg_x = k.dram_out("dbg_x", [128, 8 * T], BF16) if DEBUG else None
    dbg_s = k.dram_out("dbg_s", [DM, Tc], F32) if DEBUG else None
    st_tok = None
    guard = ()
    for ci, c0 in enumerate(range(0, T, Tc)):
        tg = matmul_fm(k, pool, w=wg, nK=8, nF=24, T=Tc,
                       rhs=lambda kc, a0, a1: xb.t(kc, c0 + a0, c0 + a1),
                       evac=lambda e, ps, fo, a0, a1: e.activation(
                           out=hb.t(fo, a0, a1), in_=ps, func=AF.Silu, scale=SC),
                       evac_eng="scalar", mm_deps=[txb])
        if DEBUG and ci == 0:
            k.dma("gpsimd", dbg_g, hb.sb[:, 0:24 * Tc], deps=[tg])
            k.dma("gpsimd", dbg_x, xb.sb[:, 0:8 * T], deps=[txb])
        tu = matmul_fm(k, pool, w=wu, nK=8, nF=24, T=Tc,
                       rhs=lambda kc, a0, a1: xb.t(kc, c0 + a0, c0 + a1),
                       evac=lambda e, ps, fo, a0, a1: e.scalar_tensor_tensor(
                           hb.t(fo, a0, a1), ps, SC, hb.t(fo, a0, a1),
                           op0=OP.mult, op1=OP.mult),
                       evac_eng="vector",
                       mm_deps=[txb, tg] + ([k.fence("gpsimd", dma=True)] if DEBUG else []))
        td = matmul_fm(k, pool, w=wd, nK=24, nF=8, T=Tc,
                       rhs=lambda kc, a0, a1: hb.t(kc, a0, a1),
                       evac=lambda e, ps, fo, a0, a1: e.activation(
                           out=s_c.t(fo, a0, a1), in_=ps, func=AF.Copy, scale=SC),
                       evac_eng="scalar", mm_deps=[tu] + ([st_tok] if st_tok else []))
        t_dbg = None
        if DEBUG and ci == 0:
            k.dma("gpsimd", dbg_h, hb.sb[:, 0:24 * Tc], deps=[tu])
            t_dbg = k.dma("gpsimd", dbg_s.rearrange("(n p) t -> p n t", p=128),
                          s_c.view3(), deps=[td])
        txc = load_fm(k, xc, x_h, src=(c0, c0 + Tc), deps=[td])
        r_toks = [k.op("vector", lambda e, j=j: e.tensor_add(
            s_c.t(j), xc.t(j), s_c.t(j)),
            deps=[txc, td, t_dbg] if j == 0 else (), inc=True)
            for j in range(8)]
        tok, _, guard = ln_chunk(k, pool, cs, r_fm=s_c, sq_bf=sq, rb_bf=rb, y_fm=s_c,
                                 yb_fm=None, stats_sb=stats, bias_sb=bias2,
                                 r_toks=r_toks, n=Tc, guard=guard)
        st_tok = store_fm(k, s_c, y_h, dst=(c0, c0 + Tc), deps=[tok], eng="sync")
        if ci < 3:
            k.queues["sync"].append(("wait", "c_vector", k.cnt["c_vector"]))
            k.hwm[("sync", "c_vector")] = k.cnt["c_vector"]
    k.op("sync", lambda e: e.nop(), deps=[k.fence("sync", dma=True)])


def l3f_inputs(xslice, d):
    return {
        "x": np.ascontiguousarray(xslice),
        "wgT": np.ascontiguousarray(d["L3_ffn_w_gate"].T),
        "wuT": np.ascontiguousarray(d["L3_ffn_w_up"].T),
        "wdT": np.ascontiguousarray(d["L3_ffn_w_down"].T),
        **const_inputs(),
    }




def build_l2f(k):
    T = TCORE
    Tc = 512
    cs = Consts(k)
    pool = PsumPool(k, 4, "pp")
    x_h = k.dram_in("x", [DM, T], F32)
    y_h = k.dram_out("y", [DM, T], F32)
    stage = StagePool(k, "stg", 32, F32)
    w1 = StreamW(k, "w1", k.dram_in("w1T", [DM, 4096], F32), DM, 4096, stage)
    w2 = StreamW(k, "w2", k.dram_in("w2T", [DM, 4096], F32), DM, 4096, stage)
    h1 = StreamW(k, "h1", k.dram_in("h1T", [4096, 4096], F32), 4096, 4096, stage)
    h2 = StreamW(k, "h2", k.dram_in("h2T", [4096, 4096], F32), 4096, 4096, stage)
    w3 = StreamW(k, "w3", k.dram_in("w3T", [4096, DM], F32), 4096, DM, stage)
    sc_h = k.dram_in("scales", [128, 32 * 4 + 8], F32)
    scs = k.sbuf("scs", [128, 32 * 4 + 8], F32)
    tsc = k.dma("sync", scs[:], sc_h)

    def scol(which, fo):
        return scs[:, which * 32 + fo: which * 32 + fo + 1]

    xc = FM(k, "xc", DM, Tc, F32)
    xbc = FM(k, "xbc", DM, Tc, BF16)
    hb1 = FM(k, "hb1", 4096, Tc, BF16)
    hb2 = FM(k, "hb2", 4096, Tc, BF16)
    s_c = FM(k, "sc", DM, Tc, F32)
    rb = xbc
    sq = xbc
    stats = k.sbuf("st", [1, 3 * Tc], F32)
    bias2 = k.sbuf("bi", [128, 2 * Tc], F32)

    st_tok = None
    guard = ()
    prev = None
    r_last = None
    for ci, c0 in enumerate(range(0, T, Tc)):
        txc = load_fm(k, xc, x_h, src=(c0, c0 + Tc),
                      deps=[r_last] if r_last else ())
        txb = cast_fm(k, xbc, xc, deps=[txc] + list(guard))
        t1 = matmul_fm(k, pool, w=w1, nK=8, nF=32, T=Tc,
                       rhs=lambda kc, a0, a1: xbc.t(kc, a0, a1),
                       evac=lambda e, ps, fo, a0, a1: e.activation(
                           out=hb1.t(fo, a0, a1), in_=ps, func=AF.Silu,
                           scale=scol(0, fo)),
                       evac_eng="scalar", mm_deps=[txb, tsc, prev])
        t2 = matmul_fm(k, pool, w=w2, nK=8, nF=32, T=Tc,
                       rhs=lambda kc, a0, a1: xbc.t(kc, a0, a1),
                       evac=lambda e, ps, fo, a0, a1: e.scalar_tensor_tensor(
                           hb1.t(fo, a0, a1), ps, scol(1, fo), hb1.t(fo, a0, a1),
                           op0=OP.mult, op1=OP.mult),
                       evac_eng="vector", mm_deps=[txb, t1])
        t3 = matmul_fm(k, pool, w=h1, nK=32, nF=32, T=Tc,
                       rhs=lambda kc, a0, a1: hb1.t(kc, a0, a1),
                       evac=lambda e, ps, fo, a0, a1: e.activation(
                           out=hb2.t(fo, a0, a1), in_=ps, func=AF.Gelu,
                           scale=scol(2, fo)),
                       evac_eng="scalar", mm_deps=[t2])
        t4 = matmul_fm(k, pool, w=h2, nK=32, nF=32, T=Tc,
                       rhs=lambda kc, a0, a1: hb2.t(kc, a0, a1),
                       evac=lambda e, ps, fo, a0, a1: e.activation(
                           out=hb1.t(fo, a0, a1), in_=ps, func=AF.Gelu,
                           scale=scol(3, fo)),
                       evac_eng="scalar", mm_deps=[t3])
        t5 = matmul_fm(k, pool, w=w3, nK=32, nF=8, T=Tc,
                       rhs=lambda kc, a0, a1: hb1.t(kc, a0, a1),
                       evac=lambda e, ps, fo, a0, a1: e.activation(
                           out=s_c.t(fo, a0, a1), in_=ps, func=AF.Copy,
                           scale=scs[:, 128 + fo:128 + fo + 1]),
                       evac_eng="scalar",
                       mm_deps=[t4] + ([st_tok] if st_tok else []))
        r_toks = [k.op("vector", lambda e, j=j: e.tensor_add(
            s_c.t(j), xc.t(j), s_c.t(j)),
            deps=[txc, t5] if j == 0 else (), inc=True)
            for j in range(8)]
        r_last = r_toks[-1]
        tok, _, guard = ln_chunk(k, pool, cs, r_fm=s_c, sq_bf=sq, rb_bf=rb,
                                 y_fm=s_c, yb_fm=None, stats_sb=stats,
                                 bias_sb=bias2, r_toks=r_toks, n=Tc, guard=guard)
        st_tok = store_fm(k, s_c, y_h, dst=(c0, c0 + Tc), deps=[tok], eng="sync")
        prev = st_tok
    k.op("sync", lambda e: e.nop(), deps=[k.fence("sync", dma=True)])


def l2f_inputs(xslice, d):
    scales = np.zeros((128, 32 * 4 + 8), np.float32)
    scales[:, 0:32] = colify(d["L2_ffn_w1_s"])
    scales[:, 32:64] = colify(d["L2_ffn_w2_s"])
    scales[:, 64:96] = colify(d["L2_ffn_h1_s"])
    scales[:, 96:128] = colify(d["L2_ffn_h2_s"])
    scales[:, 128:136] = colify(d["L2_ffn_w3_s"])
    return {
        "x": np.ascontiguousarray(xslice),
        "w1T": np.ascontiguousarray(d["L2_ffn_w1_w"].T),
        "w2T": np.ascontiguousarray(d["L2_ffn_w2_w"].T),
        "h1T": np.ascontiguousarray(d["L2_ffn_h1_w"].T),
        "h2T": np.ascontiguousarray(d["L2_ffn_h2_w"].T),
        "w3T": np.ascontiguousarray(d["L2_ffn_w3_w"].T),
        "scales": scales,
        **const_inputs(),
    }


def build_l0f(k):
    T = TCORE
    Tc = 512
    cs = Consts(k)
    pool = PsumPool(k, 4, "pp")
    x_h = k.dram_in("x", [DM, T], F32)
    y_h = k.dram_out("y", [DM, T], F32)
    stage = StagePool(k, "stg", 16, F32)
    wenc = StreamW(k, "wenc", k.dram_in("W_enc", [DM, 2048], F32), DM, 2048, stage)
    wdec = StreamW(k, "wdec", k.dram_in("W_dec", [2048, DM], F32), 2048, DM, stage)
    ws2c_h = k.dram_in("W_s2c", [2048, 64], F32)
    wr_h = k.dram_in("Wr", [64, 8], F32)
    w1_h = k.dram_in("W1", [8, 64, 1024], F32)
    w2_h = k.dram_in("W2", [8, 1024, 64], F32)
    wc2s_h = k.dram_in("W_c2s", [64, 2048], F32)
    b2_h = k.dram_in("b2", [8, 64], F32)
    cols_h = k.dram_in("cols", [128, 16 + 1 + 1 + 64 + 16 + 8 + 8 + 8], F32)
    cols = k.sbuf("colsb", [128, 122], F32)
    tcols = k.dma("sync", cols[:], cols_h)
    O_BENC, O_BS2C, O_BR, O_B1, O_BC2S, O_BDEC, O_LNG, O_LNB = 0, 16, 17, 18, 82, 98, 106, 114

    def col(o, i=0):
        return cols[:, o + i:o + i + 1]

    ws2c = k.sbuf("ws2c", [128, 16 * 64], BF16)
    wr = k.sbuf("wr", [64, 8], BF16)
    w1 = k.sbuf("w1", [64, 8 * 1024], BF16)
    w2 = k.sbuf("w2", [128, 8 * 8 * 64], BF16)
    wc2s = k.sbuf("wc2s", [64, 2048], BF16)
    b2sb = k.sbuf("b2sb", [8, 64], BF16)

    def stage_load(name, dst_sb, hbm, P, F):
        si, raw, sfree = stage.acquire()
        dst = raw[0:P, 0:F]
        if len(hbm.ap) > 2:
            dst = dst.rearrange("p (a b) -> p a b", a=hbm.ap[1][1])
        if len(hbm.ap) > 3:
            dst = dst.rearrange("p a (b c) -> p a b c", b=hbm.ap[2][1])
        t = k.dma("sync", dst, hbm, deps=[sfree] if sfree else ())
        tc = k.op("gpsimd", lambda e: e.tensor_copy(dst_sb[0:P, 0:F], raw[0:P, 0:F]),
                  deps=[t], inc=True)
        stage.release(si, tc)
        return tc

    t_ws2c = stage_load("ws2c", ws2c, ws2c_h.rearrange("(n p) f -> p n f", p=128), 128, 16 * 64)
    t_wr = stage_load("wr", wr, wr_h, 64, 8)
    for ei in range(8):
        t_w1 = stage_load("w1", _FMSlice(w1, ei * 1024, 64),
                          w1_h[ei], 64, 1024)
        t_w2 = stage_load("w2", _FMSlice(w2, ei * 512, 128),
                          w2_h[ei].rearrange("(n p) f -> p n f", p=128), 128, 512)
    t_wc2s = stage_load("wc2s", wc2s, wc2s_h, 64, 2048)
    t_b2 = stage_load("b2", b2sb, b2_h, 8, 64)
    t_small = t_b2

    xc = FM(k, "xc", DM, Tc, F32)
    xbc = FM(k, "xbc", DM, Tc, BF16)
    senc = FM(k, "senc", 2048, Tc, BF16)
    contb = FM(k, "contb", 64, Tc, BF16)
    he = FM(k, "he", 1024, Tc, BF16)
    eob = FM(k, "eob", 64, Tc, BF16)
    gadd = k.sbuf("gadd", [128, 8 * Tc], BF16)
    dec = FM(k, "dec", DM, Tc, F32)
    s_c = FM(k, "sc", DM, Tc, F32)
    rb = xbc
    lgt = k.sbuf("lgt", [8, Tc], F32)
    ptm = k.sbuf("ptm", [128, 4 * 8], F32)
    gtm = k.sbuf("gtm", [128, 4 * 8], F32)
    gem_rows = k.sbuf("gemr", [1, 8 * Tc], F32)
    gemb = k.sbuf("gemb", [8, Tc], BF16)
    gbc = k.sbuf("gbc", [128, Tc], BF16)
    stats = k.sbuf("st", [1, 3 * Tc], F32)
    bias2 = k.sbuf("bi", [128, 2 * Tc], F32)
    tmp8 = k.sbuf("tmp8", [128, 4 * 8], F32)
    tmp1 = k.sbuf("tmp1", [128, 12], F32)
    eo_ps = k.psum("eops", [64, 512])
    eo_free = None

    guard = ()
    guard2 = ()
    st_tok = None
    r_last = None
    eo_reads = None
    for ci, c0 in enumerate(range(0, T, Tc)):
        txc = load_fm(k, xc, x_h, src=(c0, c0 + Tc),
                      deps=[r_last] if r_last else ())
        txb = cast_fm(k, xbc, xc, deps=[txc] + list(guard) + list(guard2))
        t1 = matmul_fm(k, pool, w=wenc, nK=8, nF=16, T=Tc,
                       rhs=lambda kc, a0, a1: xbc.t(kc, a0, a1),
                       evac=lambda e, ps, fo, a0, a1: e.activation(
                           out=senc.t(fo, a0, a1), in_=ps, func=AF.Sigmoid,
                           bias=col(O_BENC, fo)),
                       evac_eng="scalar", mm_deps=[txb, tcols, st_tok])
        def ws2c_lhsT(kc, fo):
            return ws2c[:, kc * 64:(kc + 1) * 64]
        t2 = matmul_fm(k, pool, w=_FakeW(ws2c_lhsT, [t_ws2c]), nK=16, nF=1, T=Tc,
                       rhs=lambda kc, a0, a1: senc.t(kc, a0, a1),
                       evac=lambda e, ps, fo, a0, a1: e.tensor_scalar_add(
                           contb.t(0, a0, a1), ps, cols[0:64, O_BS2C:O_BS2C + 1]),
                       evac_eng="vector", mm_deps=[t1, t_small], M=lambda fo: 64)
        t3 = matmul_fm(k, pool, w=_FakeW(lambda kc, fo: wr[0:64, 0:8], [t_wr]),
                       nK=1, nF=1, T=Tc,
                       rhs=lambda kc, a0, a1: contb.t(0, a0, a1, r0=0, r1=64),
                       evac=lambda e, ps, fo, a0, a1: e.tensor_scalar_add(
                           lgt[0:8, a0:a1], ps, cols[0:8, O_BR:O_BR + 1]),
                       evac_eng="vector", mm_deps=[t2], M=lambda fo: 8)
        tg_toks = []
        for g in range(4):
            i1, ps1, f1 = pool.acquire()
            tt = k.op("tensor", lambda e, g=g, ps=ps1: e.transpose(
                ps[0:128, 0:8], lgt[0:8, g * 128:(g + 1) * 128], cs.idf[0:8, 0:8]),
                deps=[t3, f1, cs.tok], inc=True)
            tcp = k.op("vector", lambda e, g=g, ps=ps1: e.tensor_copy(
                ptm[:, g * 8:(g + 1) * 8], ps[0:128, 0:8]), deps=[tt], inc=True)
            pool.release(i1, tcp)
            tg_toks.append(tcp)
        for g in range(4):
            P8 = ptm[:, g * 8:(g + 1) * 8]
            mx = tmp1[:, 0:1]
            k.op("vector", lambda e, P8=P8, mx=mx: e.reduce_max(mx, P8, axis=AX.X),
                 deps=[tg_toks[g]])
            k.op("vector", lambda e, mx=mx: e.tensor_scalar_mul(tmp1[:, 1:2], mx, -1.0))
            t_sm = k.op("scalar", lambda e, P8=P8: e.activation(
                out=P8, in_=P8, func=AF.Exp, bias=tmp1[:, 1:2]),
                deps=[k.fence("vector")], inc=True)
            k.op("vector", lambda e, P8=P8: e.reduce_sum(tmp1[:, 2:3], P8, axis=AX.X),
                 deps=[t_sm])
            k.op("vector", lambda e: e.reciprocal(tmp1[:, 3:4], tmp1[:, 2:3]))
            k.op("vector", lambda e, P8=P8: e.tensor_scalar_mul(P8, P8, tmp1[:, 3:4]))
            G8 = gtm[:, g * 8:(g + 1) * 8]
            T8 = tmp8[:, g * 8:(g + 1) * 8]
            m1 = tmp1[:, 4:5]
            m2 = tmp1[:, 5:6]
            k.op("vector", lambda e, P8=P8, m1=m1: e.reduce_max(m1, P8, axis=AX.X))
            k.op("vector", lambda e, P8=P8, T8=T8, m1=m1: e.tensor_scalar(
                T8, P8, m1, None, op0=OP.is_equal))
            k.op("vector", lambda e, P8=P8, T8=T8, G8=G8: e.tensor_tensor(
                G8, T8, P8, op=OP.mult))
            k.op("vector", lambda e, P8=P8, G8=G8: e.tensor_tensor(
                P8, P8, G8, op=OP.subtract))
            k.op("vector", lambda e, P8=P8, m2=m2: e.reduce_max(m2, P8, axis=AX.X))
            k.op("vector", lambda e, m1=m1, m2=m2: e.tensor_tensor(
                tmp1[:, 6:7], m1, m2, op=OP.add))
            k.op("vector", lambda e: e.reciprocal(tmp1[:, 7:8], tmp1[:, 6:7]))
            k.op("vector", lambda e, P8=P8, T8=T8, m2=m2: e.tensor_scalar(
                T8, P8, m2, None, op0=OP.is_equal))
            k.op("vector", lambda e, P8=P8, T8=T8: e.tensor_tensor(
                T8, T8, P8, op=OP.mult))
            k.op("vector", lambda e, G8=G8, T8=T8: e.tensor_tensor(
                G8, G8, T8, op=OP.add))
            t_g = k.op("vector", lambda e, G8=G8: e.tensor_scalar_mul(
                G8, G8, tmp1[:, 7:8]))
        ge_toks = []
        for g in range(4):
            i1, ps1, f1 = pool.acquire()
            tt = k.op("tensor", lambda e, g=g, ps=ps1: e.transpose(
                ps[0:8, 0:128], gtm[:, g * 8:(g + 1) * 8], cs.idf[:, :]),
                deps=[t_g, f1], inc=True)
            tcp = k.op("vector", lambda e, g=g, ps=ps1: e.tensor_copy(
                gemb[0:8, g * 128:(g + 1) * 128], ps[0:8, 0:128]), deps=[tt], inc=True)
            pool.release(i1, tcp)
            ge_toks.append(tcp)
        t_gemb = ge_toks[-1]
        gr_toks = []
        for ei in range(8):
            for g in range(4):
                i1, ps1, f1 = pool.acquire()
                tt = k.op("tensor", lambda e, ei=ei, g=g, ps=ps1: e.transpose(
                    ps[0:1, 0:128], gtm[:, g * 8 + ei:g * 8 + ei + 1], cs.idf[:, :]),
                    deps=[t_g, f1], inc=True)
                tcp = k.op("vector", lambda e, ei=ei, g=g, ps=ps1: e.tensor_copy(
                    gem_rows[0:1, ei * Tc + g * 128: ei * Tc + (g + 1) * 128],
                    ps[0:1, 0:128]), deps=[tt], inc=True)
                pool.release(i1, tcp)
                gr_toks.append(tcp)
        ge_toks = ge_toks + gr_toks
        mm_eo = k.op("tensor", lambda e, ps=eo_ps: e.matmul(
            ps[0:64, 0:Tc], b2sb[0:8, 0:64], gemb[0:8, 0:Tc],
            start=True, stop=False), deps=[t_gemb, t_small, eo_free, eo_reads],
            inc=False)
        for ei in range(8):
            the = matmul_fm(k, pool, w=_FakeW(
                lambda kc, fo, ei=ei: w1[0:64, ei * 1024 + fo * 128: ei * 1024 + (fo + 1) * 128],
                [t_small]), nK=1, nF=8, T=Tc,
                rhs=lambda kc, a0, a1: contb.t(0, a0, a1, r0=0, r1=64),
                evac=lambda e, ps, fo, a0, a1, ei=ei: e.activation(
                    out=he.t(fo, a0, a1), in_=ps, func=AF.Relu,
                    bias=col(O_B1, ei * 8 + fo)),
                evac_eng="scalar", mm_deps=[t2, t_gemb])
            ib, psb, fb = pool.acquire()
            tb = k.op("tensor", lambda e, ps=psb, ei=ei: e.matmul(
                ps[0:128, 0:Tc], cs.or_[0:1, :],
                gem_rows[0:1, ei * Tc:(ei + 1) * Tc],
                start=True, stop=True), deps=[fb] + ge_toks, inc=True)
            tbc = k.op("vector", lambda e, ps=psb: e.tensor_copy(
                gbc[:, 0:Tc], ps[0:128, 0:Tc]), deps=[tb], inc=True)
            pool.release(ib, tbc)
            sc_toks = [k.op("vector", lambda e, j=j: e.tensor_tensor(
                he.t(j), he.t(j), gbc[:, 0:Tc], op=OP.mult),
                deps=[the, tbc] if j == 0 else (), inc=True) for j in range(8)]
            for kc in range(8):
                last = (ei == 7 and kc == 7)
                mm_eo = k.op("tensor", lambda e, ps=eo_ps, ei=ei, kc=kc, last=last:
                             e.matmul(ps[0:64, 0:Tc],
                                      w2[:, (ei * 8 + kc) * 64:(ei * 8 + kc + 1) * 64],
                                      he.t(kc), start=False, stop=last),
                             deps=sc_toks if kc == 0 else (), inc=last)
        t_eo = k.op("vector", lambda e, ps=eo_ps: e.tensor_copy(
            eob.t(0), ps[0:64, 0:Tc]), deps=[mm_eo], inc=True)
        eo_free = t_eo
        t6 = matmul_fm(k, pool, w=_FakeW(
            lambda kc, fo: wc2s[0:64, fo * 128:(fo + 1) * 128], [t_wc2s]),
            nK=1, nF=16, T=Tc,
            rhs=lambda kc, a0, a1: eob.t(0, a0, a1, r0=0, r1=64),
            evac=lambda e, ps, fo, a0, a1: e.activation(
                out=senc.t(fo, a0, a1), in_=ps, func=AF.Sigmoid,
                bias=col(O_BC2S, fo)),
            evac_eng="scalar", mm_deps=[t_eo, t1])
        eo_reads = t6
        t7 = matmul_fm(k, pool, w=wdec, nK=16, nF=8, T=Tc,
                       rhs=lambda kc, a0, a1: senc.t(kc, a0, a1),
                       evac=lambda e, ps, fo, a0, a1: e.tensor_scalar_add(
                           dec.t(fo, a0, a1), ps, col(O_BDEC, fo)),
                       evac_eng="vector", mm_deps=[t6])
        d_toks = [t7] * 8
        tok_in, _, guard = ln_chunk(k, pool, cs, r_fm=dec, sq_bf=rb, rb_bf=rb,
                                    y_fm=s_c, yb_fm=None, stats_sb=stats,
                                    bias_sb=bias2, r_toks=d_toks, n=Tc,
                                    guard=guard,
                                    gcol=lambda j: col(O_LNG, j),
                                    bcol=lambda j: col(O_LNB, j))
        r_toks = [k.op("vector", lambda e, j=j: e.tensor_add(
            s_c.t(j), xc.t(j), s_c.t(j)),
            deps=[tok_in, txc] if j == 0 else (), inc=True) for j in range(8)]
        r_last = r_toks[-1]
        tok2, _, guard2 = ln_chunk(k, pool, cs, r_fm=s_c, sq_bf=rb, rb_bf=rb,
                                   y_fm=s_c, yb_fm=None, stats_sb=stats,
                                   bias_sb=bias2, r_toks=r_toks, n=Tc,
                                   guard=guard2)
        st_tok = store_fm(k, s_c, y_h, dst=(c0, c0 + Tc), deps=[tok2], eng="sync")
    k.op("sync", lambda e: e.nop(), deps=[k.fence("sync", dma=True)])


class ViewFM:

    def __init__(self, sb, P, T):
        self.sb, self.P, self.T = sb, P, T
        self.nt = (P + 127) // 128

    def rows(self, j):
        return min(128, self.P - j * 128)

    def t(self, j, c0=0, c1=None, r0=0, r1=None):
        c1 = self.T if c1 is None else c1
        r1 = self.rows(j) if r1 is None else r1
        return self.sb[r0:r1, j * self.T + c0: j * self.T + c1]

    def view3(self, c0=0, c1=None):
        c1 = self.T if c1 is None else c1
        ap = self.sb[:, 0:self.nt * self.T].rearrange("p (n t) -> p n t", n=self.nt)
        return ap[:, :, c0:c1]


class _FMSlice:
    def __init__(self, sb, off, P):
        self.sb, self.off, self.P = sb, off, P

    def __getitem__(self, idx):
        rows, cols = idx
        return self.sb[rows, self.off + (cols.start or 0):
                       self.off + cols.stop]


class _FakeW:
    def __init__(self, lhsT, toks):
        self._l = lhsT
        self._t = toks

    def prep(self, k, fo):
        return self._l, list(self._t)

    def done(self, k, fo, mm_tok):
        pass


def l0f_inputs(xslice, d):
    cols = np.zeros((128, 122), np.float32)
    cols[:, 0:16] = colify(d["L0_ffn_b_enc"])
    cols[0:64, 16] = d["L0_ffn_b_s2c"]
    cols[0:8, 17] = d["L0_ffn_br"]
    b1 = d["L0_ffn_b1"]
    for e in range(8):
        cols[:, 18 + e * 8: 18 + (e + 1) * 8] = colify(b1[e])
    cols[:, 82:98] = colify(d["L0_ffn_b_c2s"])
    cols[:, 98:106] = colify(d["L0_ffn_b_dec"])
    cols[:, 106:114] = colify(d["L0_ffn_ln_g"])
    cols[:, 114:122] = colify(d["L0_ffn_ln_b"])
    return {
        "x": np.ascontiguousarray(xslice),
        "W_enc": np.ascontiguousarray(d["L0_ffn_W_enc"]),
        "W_dec": np.ascontiguousarray(d["L0_ffn_W_dec"]),
        "W_s2c": np.ascontiguousarray(d["L0_ffn_W_s2c"]),
        "Wr": np.ascontiguousarray(d["L0_ffn_Wr"]),
        "W1": np.ascontiguousarray(d["L0_ffn_W1"]),
        "W2": np.ascontiguousarray(d["L0_ffn_W2"]),
        "W_c2s": np.ascontiguousarray(d["L0_ffn_W_c2s"]),
        "b2": np.ascontiguousarray(d["L0_ffn_b2"]),
        "cols": cols,
        **const_inputs(),
    }


def _l0_common(k, cs, pool, stage, x_h, Tc, want_res, want_C, delta_dtype,
               x_chunked=False):
    T = TCORE
    wip_xm = StreamW(k, "wipx", k.dram_in("WipXm", [DM, 2048], F32), DM, 2048, stage)
    wdt_h = k.dram_in("WdtT", [2048, 64], F32)
    wB_h = k.dram_in("WBT", [2048, 16], F32)
    wC_h = k.dram_in("WCT", [2048, 16], F32)
    wdtp_h = k.dram_in("WdtpT", [64, 2048], F32)
    sel16_h = k.dram_in("sel16", [16, 16 * 128], BF16)
    cols_h = k.dram_in("l0cols", [128, 16 * 4 + 16 + 16 + 256 + 16 + 16], F32)
    colsb = k.sbuf("l0colsb", [128, 16 * 4 + 16 + 16 + 256 + 16 + 16], F32)
    tcols = k.dma("sync", colsb[:], cols_h)
    O_TAP, O_CB, O_DTB, O_AL, O_D = 0, 64, 80, 96, 352

    def col(o, i=0):
        return colsb[:, o + i:o + i + 1]

    Acols = k.sbuf("acols", [128, 256], F32)
    t_a = k.op("scalar", lambda e: e.activation(
        out=Acols[:], in_=colsb[:, O_AL:O_AL + 256], func=AF.Exp), deps=[tcols], inc=True)
    t_a = k.op("vector", lambda e: e.tensor_scalar_mul(Acols[:], Acols[:], -1.0),
               deps=[t_a], inc=True)

    wdt = k.sbuf("wdt", [128, 16 * 64], BF16)
    wB = k.sbuf("wB", [128, 16 * 16], BF16)
    wC = k.sbuf("wC", [128, 16 * 16], BF16)
    wdtp = k.sbuf("wdtp", [64, 2048], BF16)
    sel16 = k.sbuf("sel16b", [16, 16 * 128], BF16)
    t_sel = k.dma("sync", sel16[:], sel16_h)

    def stage_load(dst_sb, hbm, P, F):
        si, raw, sfree = stage.acquire()
        dst = raw[0:P, 0:F]
        if len(hbm.ap) > 2:
            dst = dst.rearrange("p (a b) -> p a b", a=hbm.ap[1][1])
        t = k.dma("sync", dst, hbm, deps=[sfree] if sfree else ())
        tc = k.op("gpsimd", lambda e: e.tensor_copy(dst_sb[0:P, 0:F], raw[0:P, 0:F]),
                  deps=[t], inc=True)
        stage.release(si, tc)
        return tc

    stage_load(wdt, wdt_h.rearrange("(n p) f -> p n f", p=128), 128, 16 * 64)
    stage_load(wB, wB_h.rearrange("(n p) f -> p n f", p=128), 128, 16 * 16)
    if want_C:
        stage_load(wC, wC_h.rearrange("(n p) f -> p n f", p=128), 128, 16 * 16)
    t_small = stage_load(wdtp, wdtp_h, 64, 2048)

    xbe = FM(k, "xbe", DM, (Tc + 3) if x_chunked else (T + 3), BF16)
    xcl = FM(k, "xcl", DM, Tc + 3 if x_chunked else Tc, F32)
    xm_c = FM(k, "xmc", 2048, Tc + 3, BF16)
    u_c = FM(k, "uc", 2048, Tc, BF16)
    du_c = FM(k, "duc", 2048, Tc, BF16)
    sres_c = FM(k, "sres", 2048, Tc, BF16) if want_res else None
    del_c = FM(k, "delc", 2048, Tc, delta_dtype)
    acc_c = k.sbuf("accc", [128, Tc], F32)
    dtb = k.sbuf("dtb", [64, Tc], BF16)
    Bb = k.sbuf("Bb", [16, Tc], BF16)
    Cb = k.sbuf("Cb", [16, Tc], BF16) if want_C else None
    wres = (StreamW(k, "wipr", k.dram_in("WipRes", [DM, 2048], F32), DM, 2048, stage)
            if want_res else None)

    state = {"prev_xm": None, "guard": None, "txbe": None}
    if not x_chunked:
        tx = None
        for c0 in range(0, T + 3, Tc):
            c1 = min(T + 3, c0 + Tc)
            t = load_fm(k, xcl, x_h, src=(c0, c1), dst=(0, c1 - c0),
                        deps=[tx] if tx else ())
            for j in range(8):
                tx = k.op("vector", lambda e, j=j, c0=c0, c1=c1: e.tensor_copy(
                    xbe.t(j, c0, c1), xcl.t(j, 0, c1 - c0)),
                    deps=[t] if j == 0 else (), inc=(j == 7))
        state["txbe"] = tx

    def chunk_pipeline(ci, c0):
        if x_chunked:
            t = load_fm(k, xcl, x_h, src=(c0, c0 + Tc + 3),
                        deps=[state["txbe"], state.get("xguard")])
            tx = None
            for j in range(8):
                tx = k.op("vector", lambda e, j=j: e.tensor_copy(
                    xbe.t(j, 0, Tc + 3), xcl.t(j, 0, Tc + 3)),
                    deps=[t] if j == 0 else (), inc=(j == 7))
            state["txbe"] = tx
            xoff = 0
        else:
            xoff = c0
        txbe = state["txbe"]
        if ci == 0:
            t_halo = matmul_fm(k, pool, w=wip_xm, nK=8, nF=16, T=3, Tc=3,
                               rhs=lambda kc, a0, a1, xoff=xoff: xbe.t(kc, xoff + a0, xoff + a1),
                               evac=lambda e, ps, fo, a0, a1: e.activation(
                                   out=xm_c.t(fo, a0, a1), in_=ps, func=AF.Copy),
                               evac_eng="scalar", mm_deps=[txbe, state["guard"]])
        else:
            t_halo = None
            for j in range(16):
                t_halo = k.op("vector", lambda e, j=j: e.tensor_copy(
                    xm_c.t(j, 0, 3), xm_c.t(j, Tc, Tc + 3)),
                    deps=[state["prev_xm"], state["guard"]] if j == 0 else (),
                    inc=(j == 15))
        t_xm = matmul_fm(k, pool, w=wip_xm, nK=8, nF=16, T=Tc,
                         rhs=lambda kc, a0, a1, xoff=xoff: xbe.t(kc, xoff + 3 + a0, xoff + 3 + a1),
                         evac=lambda e, ps, fo, a0, a1: e.activation(
                             out=xm_c.t(fo, 3 + a0, 3 + a1), in_=ps, func=AF.Copy),
                         evac_eng="scalar", mm_deps=[txbe, t_halo])
        state["prev_xm"] = t_xm
        t_res = None
        if want_res:
            t_res = matmul_fm(k, pool, w=wres, nK=8, nF=16, T=Tc,
                              rhs=lambda kc, a0, a1, xoff=xoff: xbe.t(kc, xoff + 3 + a0, xoff + 3 + a1),
                              evac=lambda e, ps, fo, a0, a1: e.activation(
                                  out=sres_c.t(fo, a0, a1), in_=ps, func=AF.Silu),
                              evac_eng="scalar", mm_deps=[txbe])
        u_toks = []
        tu = None
        for j in range(16):
            t1 = k.op("vector", lambda e, j=j: e.tensor_scalar_mul(
                acc_c[:, 0:Tc], xm_c.t(j, 0, Tc), col(O_TAP, j)),
                deps=([t_xm, tcols] if j == 0 else []) + ([tu] if tu else []),
                inc=False)
            for tap in range(1, 4):
                t1 = k.op("vector", lambda e, j=j, tap=tap: e.scalar_tensor_tensor(
                    acc_c[:, 0:Tc], xm_c.t(j, tap, tap + Tc), col(O_TAP, 16 * tap + j),
                    acc_c[:, 0:Tc], op0=OP.mult, op1=OP.add), inc=(tap == 3))
            tu = k.op("scalar", lambda e, j=j: e.activation(
                out=u_c.t(j), in_=acc_c[:, 0:Tc], func=AF.Silu, bias=col(O_CB, j)),
                deps=[t1], inc=True)
            u_toks.append(tu)
        t_u = u_toks[-1]
        t_dt = matmul_fm(k, pool, w=_FakeW(
            lambda kc, fo: wdt[:, kc * 64:(kc + 1) * 64], [t_small]),
            nK=16, nF=1, T=Tc, M=lambda fo: 64,
            rhs=lambda kc, a0, a1: u_c.t(kc, a0, a1),
            evac=lambda e, ps, fo, a0, a1: e.activation(
                out=dtb[0:64, a0:a1], in_=ps, func=AF.Copy),
            evac_eng="scalar", mm_deps=u_toks)
        t_B = matmul_fm(k, pool, w=_FakeW(
            lambda kc, fo: wB[:, kc * 16:(kc + 1) * 16], [t_small]),
            nK=16, nF=1, T=Tc, M=lambda fo: 16,
            rhs=lambda kc, a0, a1: u_c.t(kc, a0, a1),
            evac=lambda e, ps, fo, a0, a1: e.activation(
                out=Bb[0:16, a0:a1], in_=ps, func=AF.Copy),
            evac_eng="scalar", mm_deps=[t_u])
        t_C = None
        if want_C:
            t_C = matmul_fm(k, pool, w=_FakeW(
                lambda kc, fo: wC[:, kc * 16:(kc + 1) * 16], [t_small]),
                nK=16, nF=1, T=Tc, M=lambda fo: 16,
                rhs=lambda kc, a0, a1: u_c.t(kc, a0, a1),
                evac=lambda e, ps, fo, a0, a1: e.activation(
                    out=Cb[0:16, a0:a1], in_=ps, func=AF.Copy),
                evac_eng="scalar", mm_deps=[t_u])
        t_del = None
        for fo in range(16):
            i, ps, ftok = pool.acquire()
            mm = k.op("tensor", lambda e, fo=fo, ps=ps: e.matmul(
                ps[0:128, 0:Tc], wdtp[0:64, fo * 128:(fo + 1) * 128], dtb[0:64, 0:Tc],
                start=True, stop=True), deps=[t_dt, t_small, ftok], inc=True)
            t1 = k.op("scalar", lambda e, ps=ps, fo=fo: e.activation(
                out=acc_c[:, 0:Tc], in_=ps[0:128, 0:Tc], func=AF.Abs,
                bias=col(O_DTB, fo)), deps=[mm, t_del], inc=True)
            t2 = k.op("scalar", lambda e: e.activation(
                out=acc_c[:, 0:Tc], in_=acc_c[:, 0:Tc], func=AF.Exp, scale=-1.0),
                deps=[t1], inc=True)
            t3 = k.op("scalar", lambda e: e.activation(
                out=acc_c[:, 0:Tc], in_=acc_c[:, 0:Tc], func=AF.Ln, bias=1.0),
                deps=[t2], inc=True)
            t4 = k.op("vector", lambda e, ps=ps, fo=fo: e.tensor_scalar(
                del_c.t(fo), ps[0:128, 0:Tc], col(O_DTB, fo), 0.0,
                op0=OP.add, op1=OP.max), deps=[t3, mm], inc=True)
            t_del = k.op("vector", lambda e, fo=fo: e.tensor_tensor(
                del_c.t(fo), del_c.t(fo), acc_c[:, 0:Tc], op=OP.add), inc=True)
            pool.release(i, t_del)
        t_du = None
        for j in range(16):
            t_du = k.op("vector", lambda e, j=j: e.tensor_tensor(
                du_c.t(j), del_c.t(j), u_c.t(j), op=OP.mult),
                deps=[t_del] if j == 0 else (), inc=(j == 15))
        state["xguard"] = t_res if t_res is not None else t_xm
        return dict(t_xm=t_xm, t_u=t_u, t_B=t_B, t_C=t_C, t_del=t_del,
                    t_res=t_res, u_toks=u_toks, t_du=t_du)

    return dict(chunk=chunk_pipeline, xbe=xbe, xcl=xcl, u_c=u_c, del_c=del_c,
                du_c=du_c, sres_c=sres_c, Bb=Bb, Cb=Cb, Acols=Acols, sel16=sel16,
                t_sel=t_sel, col=col, O_D=O_D, colsb=colsb, state=state)


def build_l0a(k):
    T, Tc = TCORE, 512
    cs = Consts(k)
    pool = PsumPool(k, 4, "pp")
    stage = StagePool(k, "stg", 16, F32)
    x_h = k.dram_in("x", [DM, T + 3], F32)
    hA_h = k.dram_out("hA", [128, 256], F32)
    cdsum_h = k.dram_out("cdsum", [128, 16], F32)
    C = _l0_common(k, cs, pool, stage, x_h, Tc, want_res=False, want_C=False,
                   delta_dtype=BF16)
    scol = k.sbuf("scol", [128, 256], F32)
    cdsum = k.sbuf("cdsumb", [128, 16], F32)
    tz = k.op("vector", lambda e: e.memset(scol[:], 0.0), inc=True)
    tz2 = k.op("vector", lambda e: e.memset(cdsum[:], 0.0), inc=True)
    bbc = k.sbuf("bbc", [128, Tc], BF16)
    dA = k.sbuf("dA", [128, Tc], BF16)
    dBu = k.sbuf("dBu", [128, Tc], BF16)
    hsc = k.sbuf("hsc", [128, Tc], BF16)
    red = k.sbuf("red", [128, 1], F32)

    for ci, c0 in enumerate(range(0, T, Tc)):
        P = C["chunk"](ci, c0)
        for j in range(16):
            t1 = k.op("vector", lambda e, j=j: e.tensor_reduce(
                red[:], C["del_c"].t(j), axis=AX.X, op=OP.add),
                deps=[P["t_del"]] if j == 0 else (), inc=False)
            k.op("vector", lambda e, j=j: e.tensor_tensor(
                cdsum[:, j:j + 1], cdsum[:, j:j + 1], red[:], op=OP.add),
                deps=[tz2] if (ci == 0 and j == 0) else (), inc=(j == 15))
        t_scan = None
        for n in range(16):
            ib, psb, fb = pool.acquire()
            tb = k.op("tensor", lambda e, n=n, ps=psb: e.matmul(
                ps[0:128, 0:Tc], C["sel16"][0:16, n * 128:(n + 1) * 128],
                C["Bb"][0:16, 0:Tc], start=True, stop=True),
                deps=[P["t_B"], C["t_sel"], fb], inc=True)
            tbc = k.op("vector", lambda e, ps=psb: e.tensor_copy(
                bbc[:, 0:Tc], ps[0:128, 0:Tc]), deps=[tb], inc=True)
            pool.release(ib, tbc)
            for j in range(16):
                ct = n * 16 + j
                ta = k.op("scalar", lambda e, j=j, ct=ct: e.activation(
                    out=dA[:, 0:Tc], in_=C["del_c"].t(j), func=AF.Exp,
                    scale=C["Acols"][:, ct:ct + 1]),
                    deps=[P["t_del"], tbc, t_scan], inc=True)
                td = k.op("vector", lambda e, j=j: e.tensor_tensor(
                    dBu[:, 0:Tc], C["du_c"].t(j), bbc[:, 0:Tc], op=OP.mult),
                    deps=[P["t_du"], tbc, ta], inc=True)
                ts_ = k.op("vector", lambda e, ct=ct: e.tensor_tensor_scan(
                    hsc[:, 0:Tc], dA[:, 0:Tc], dBu[:, 0:Tc], scol[:, ct:ct + 1],
                    op0=OP.mult, op1=OP.add), deps=[ta, tz], inc=True)
                t_scan = k.op("vector", lambda e, ct=ct: e.tensor_copy(
                    scol[:, ct:ct + 1], hsc[:, Tc - 1:Tc]), inc=True)
    t1 = k.dma("sync", hA_h, scol[:], deps=[t_scan])
    t2 = k.dma("sync", cdsum_h, cdsum[:], deps=[k.fence("vector")])
    k.op("sync", lambda e: e.nop(), deps=[k.fence("sync", dma=True)])


def l0a_inputs(xslice_halo, d):
    wip = np.ascontiguousarray(d["L0_attn_in_proj_w"].T)
    xp = d["L0_attn_x_proj_w"]
    cols = np.zeros((128, 16 * 4 + 16 + 16 + 256 + 16 + 16), np.float32)
    cw = d["L0_attn_conv_w"][:, 0, :]
    for tap in range(4):
        cols[:, 16 * tap:16 * (tap + 1)] = colify(cw[:, tap])
    cols[:, 64:80] = colify(d["L0_attn_conv_b"])
    cols[:, 80:96] = colify(d["L0_attn_dt_proj_b"])
    al = d["L0_attn_A_log"]
    alc = np.zeros((128, 256), np.float32)
    for n in range(16):
        for j in range(16):
            alc[:, n * 16 + j] = al[j * 128:(j + 1) * 128, n]
    cols[:, 96:352] = alc
    cols[:, 352:368] = colify(d["L0_attn_D"])
    sel = np.zeros((16, 16 * 128), np.float32)
    for n in range(16):
        sel[n, n * 128:(n + 1) * 128] = 1.0
    return {
        "x": np.ascontiguousarray(xslice_halo),
        "WipXm": np.ascontiguousarray(wip[:, 0:2048]),
        "WdtT": np.ascontiguousarray(xp[0:64].T),
        "WBT": np.ascontiguousarray(xp[64:80].T),
        "WCT": np.ascontiguousarray(xp[80:96].T),
        "WdtpT": np.ascontiguousarray(d["L0_attn_dt_proj_w"].T),
        "sel16": sel.astype(bf16()),
        "l0cols": cols,
        **const_inputs(),
    }


def build_l0b(k):
    T, Tc = TCORE, 256
    cs = Consts(k)
    pool = PsumPool(k, 4, "pp")
    stage = StagePool(k, "stg", 32, F32)
    x_h = k.dram_in("x", [DM, T + 3], F32)
    c_h = k.dram_out("c", [DM, T], F32)
    part_h = k.dram_out("part", [128, 2], F32)
    pcd_h = k.dram_in("pred_cdsum", [128, 16 * 7], F32)
    phA_h = k.dram_in("pred_hA", [128, 256 * 7], F32)
    wop = StreamW(k, "wop", k.dram_in("WopT", [2048, DM], F32), 2048, DM, stage)
    wf1 = StreamW(k, "wf1", k.dram_in("Wf1T", [DM, 4096], F32), DM, 4096, stage)
    wf2 = StreamW(k, "wf2", k.dram_in("Wf2T", [4096, DM], F32), 4096, DM, stage)
    fcols_h = k.dram_in("fcols", [128, 32 + 8], F32)
    fcols = k.sbuf("fcolsb", [128, 40], F32)
    t_fc = k.dma("sync", fcols[:], fcols_h)

    C = _l0_common(k, cs, pool, stage, x_h, Tc, want_res=True, want_C=True,
                   delta_dtype=BF16, x_chunked=True)
    pcd = k.sbuf("pcd", [128, 16 * 7], F32)
    phA = k.sbuf("phA", [128, 256 * 7], F32)
    t_p1 = k.dma("sync", pcd[:], pcd_h)
    t_p2 = k.dma("sync", phA[:], phA_h)
    scol = k.sbuf("scol", [128, 256], F32)
    darg = k.sbuf("darg", [128, 256], F32)
    tz = k.op("vector", lambda e: e.memset(scol[:], 0.0), inc=True)
    for r in range(7):
        cdr = pcd[:, r * 16:(r + 1) * 16]
        cdr_rep = bass.AP(cdr.tensor, cdr.offset,
                          [list(cdr.ap[0]), [0, 16], list(cdr.ap[1])])
        k.op("vector", lambda e, cdr_rep=cdr_rep: e.tensor_tensor(
            darg.rearrange("p (n j) -> p n j", n=16), C["Acols"].rearrange(
                "p (n j) -> p n j", n=16), cdr_rep, op=OP.mult),
            deps=[t_p1, tz] if r == 0 else ())
        t_e = k.op("scalar", lambda e: e.activation(out=darg[:], in_=darg[:],
                                                    func=AF.Exp),
                   deps=[k.fence("vector")], inc=True)
        k.op("vector", lambda e: e.tensor_tensor(scol[:], scol[:], darg[:],
                                                 op=OP.mult), deps=[t_e])
        t_h0 = k.op("vector", lambda e, r=r: e.tensor_tensor(
            scol[:], scol[:], phA[:, r * 256:(r + 1) * 256], op=OP.add),
            deps=[t_p2], inc=True)

    bbc = k.sbuf("bbc", [128, Tc], BF16)
    cbc = k.sbuf("cbc", [128, Tc], BF16)
    dA = k.sbuf("dA", [128, Tc], BF16)
    dBu = k.sbuf("dBu", [128, Tc], BF16)
    hsc = k.sbuf("hsc", [128, Tc], BF16)
    y_c = FM(k, "yc", 2048, Tc, F32)
    yb_c = C["u_c"]
    mb_c = FM(k, "mbc", DM, Tc, BF16)
    m_c = FM(k, "mc", DM, Tc, F32)
    fh = FM(k, "fh", 4096, Tc, BF16)
    parts = k.sbuf("parts", [128, 4], F32)
    psq = k.sbuf("psq", [128, 256], F32)
    red = k.sbuf("red", [128, 1], F32)
    tzp = k.op("vector", lambda e: e.memset(parts[:], 0.0), inc=True)

    st_tok = None
    for ci, c0 in enumerate(range(0, T, Tc)):
        P = C["chunk"](ci, c0)
        t_scan = t_h0 if ci == 0 else t_scan
        yz = k.op("vector", lambda e: e.memset(y_c.sb[:], 0.0),
                  deps=[st_tok], inc=True)
        for n in range(16):
            ib, psb, fb = pool.acquire()
            tb = k.op("tensor", lambda e, n=n, ps=psb: e.matmul(
                ps[0:128, 0:Tc], C["sel16"][0:16, n * 128:(n + 1) * 128],
                C["Bb"][0:16, 0:Tc], start=True, stop=True),
                deps=[P["t_B"], C["t_sel"], fb], inc=True)
            tbc = k.op("vector", lambda e, ps=psb: e.tensor_copy(
                bbc[:, 0:Tc], ps[0:128, 0:Tc]), deps=[tb], inc=True)
            pool.release(ib, tbc)
            ic, psc, fc2 = pool.acquire()
            tb2 = k.op("tensor", lambda e, n=n, ps=psc: e.matmul(
                ps[0:128, 0:Tc], C["sel16"][0:16, n * 128:(n + 1) * 128],
                C["Cb"][0:16, 0:Tc], start=True, stop=True),
                deps=[P["t_C"], C["t_sel"], fc2], inc=True)
            tcc = k.op("vector", lambda e, ps=psc: e.tensor_copy(
                cbc[:, 0:Tc], ps[0:128, 0:Tc]), deps=[tb2], inc=True)
            pool.release(ic, tcc)
            for j in range(16):
                ct = n * 16 + j
                ta = k.op("scalar", lambda e, j=j, ct=ct: e.activation(
                    out=dA[:, 0:Tc], in_=C["del_c"].t(j), func=AF.Exp,
                    scale=C["Acols"][:, ct:ct + 1]),
                    deps=[P["t_del"], tbc, tcc, t_scan], inc=True)
                td = k.op("vector", lambda e, j=j: e.tensor_tensor(
                    dBu[:, 0:Tc], C["du_c"].t(j), bbc[:, 0:Tc], op=OP.mult),
                    deps=[P["t_du"], tbc, ta], inc=True)
                ts_ = k.op("vector", lambda e, ct=ct: e.tensor_tensor_scan(
                    hsc[:, 0:Tc], dA[:, 0:Tc], dBu[:, 0:Tc], scol[:, ct:ct + 1],
                    op0=OP.mult, op1=OP.add), deps=[ta], inc=True)
                k.op("vector", lambda e, ct=ct: e.tensor_copy(
                    scol[:, ct:ct + 1], hsc[:, Tc - 1:Tc]), inc=False)
                tch = k.op("vector", lambda e: e.tensor_tensor(
                    dBu[:, 0:Tc], hsc[:, 0:Tc], cbc[:, 0:Tc], op=OP.mult),
                    inc=True)
                t_scan = k.op("gpsimd", lambda e, j=j: e.tensor_tensor(
                    y_c.t(j), y_c.t(j), dBu[:, 0:Tc], op=OP.add),
                    deps=[tch, yz], inc=True)
        yb_toks = []
        for j in range(16):
            t1 = k.op("vector", lambda e, j=j: e.scalar_tensor_tensor(
                y_c.t(j), C["u_c"].t(j), C["col"](C["O_D"], j), y_c.t(j),
                op0=OP.mult, op1=OP.add),
                deps=[t_scan, P["t_res"]] if j == 0 else (), inc=False)
            t2 = k.op("vector", lambda e, j=j: e.tensor_tensor(
                yb_c.t(j), y_c.t(j), C["sres_c"].t(j), op=OP.mult), inc=True)
            yb_toks.append(t2)
        t_m = matmul_fm(k, pool, w=wop, nK=16, nF=8, T=Tc,
                        rhs=lambda kc, a0, a1: yb_c.t(kc, a0, a1),
                        evac=lambda e, ps, fo, a0, a1: e.activation(
                            out=m_c.t(fo, a0, a1), in_=ps, func=AF.Copy),
                        evac_eng="scalar", mm_deps=yb_toks)
        t_mb = cast_fm(k, mb_c, m_c, deps=[t_m])
        t_f1 = matmul_fm(k, pool, w=wf1, nK=8, nF=32, T=Tc,
                         rhs=lambda kc, a0, a1: mb_c.t(kc, a0, a1),
                         evac=lambda e, ps, fo, a0, a1: e.activation(
                             out=fh.t(fo, a0, a1), in_=ps, func=AF.Gelu,
                             bias=fcols[:, fo:fo + 1]),
                         evac_eng="scalar", mm_deps=[t_mb, t_fc])
        def cevac(e, ps, fo, a0, a1):
            return e.scalar_tensor_tensor(m_c.t(fo, a0, a1), ps,
                                          fcols[:, 32 + fo:32 + fo + 1],
                                          m_c.t(fo, a0, a1), op0=OP.add, op1=OP.add)
        t_c = matmul_fm(k, pool, w=wf2, nK=32, nF=8, T=Tc,
                        rhs=lambda kc, a0, a1: fh.t(kc, a0, a1),
                        evac=cevac, evac_eng="vector", mm_deps=[t_f1])
        for j in range(8):
            k.op("vector", lambda e, j=j: e.tensor_reduce(
                red[:], m_c.t(j), axis=AX.X, op=OP.add),
                deps=[t_c] if j == 0 else ())
            k.op("vector", lambda e: e.tensor_tensor(
                parts[:, 0:1], parts[:, 0:1], red[:], op=OP.add),
                deps=[tzp] if (ci == 0 and j == 0) else ())
            k.op("vector", lambda e, j=j: e.tensor_tensor(
                psq[:, 0:Tc], m_c.t(j), m_c.t(j), op=OP.mult))
            k.op("vector", lambda e: e.tensor_reduce(
                red[:], psq[:, 0:Tc], axis=AX.X, op=OP.add))
            t_part = k.op("vector", lambda e: e.tensor_tensor(
                parts[:, 1:2], parts[:, 1:2], red[:], op=OP.add), inc=True)
        st_tok = store_fm(k, m_c, c_h, dst=(c0, c0 + Tc), deps=[t_c, t_part],
                          eng="sync")
    k.dma("sync", part_h, parts[:, 0:2], deps=[t_part])
    k.op("sync", lambda e: e.nop(), deps=[k.fence("sync", dma=True)])


def l0b_inputs(xslice_halo, d, pred_cdsum, pred_hA):
    base = l0a_inputs(xslice_halo, d)
    wip = np.ascontiguousarray(d["L0_attn_in_proj_w"].T)
    fcols = np.zeros((128, 40), np.float32)
    fcols[:, 0:32] = colify(d["L0_attn_ffn_b1"])
    fcols[:, 32:40] = colify(d["L0_attn_ffn_b2"])
    base.update({
        "WipRes": np.ascontiguousarray(wip[:, 2048:4096]),
        "WopT": np.ascontiguousarray(d["L0_attn_out_proj_w"].T),
        "Wf1T": np.ascontiguousarray(d["L0_attn_ffn_w1"].T),
        "Wf2T": np.ascontiguousarray(d["L0_attn_ffn_w2"].T),
        "fcols": fcols,
        "pred_cdsum": np.ascontiguousarray(pred_cdsum),
        "pred_hA": np.ascontiguousarray(pred_hA),
    })
    return base


def build_l0c(k):
    T, Tc = TCORE, 512
    cs = Consts(k)
    pool = PsumPool(k, 4, "pp")
    x_h = k.dram_in("x", [DM, T], F32)
    c_h = k.dram_in("cin", [DM, T], F32)
    y_h = k.dram_out("y", [DM, T], F32)
    ap_h = k.dram_in("allparts", [128, 16], F32)
    ap = k.sbuf("ap", [128, 16], F32)
    t_ap = k.dma("sync", ap[:], ap_h)
    gs = k.sbuf("gs", [128, 8], F32)
    g2 = k.sbuf("g2", [2, 130], F32)
    N = float(TFULL) * DM

    k.op("vector", lambda e: e.tensor_tensor(gs[:, 0:1], ap[:, 0:1], ap[:, 2:3],
                                             op=OP.add), deps=[t_ap])
    k.op("vector", lambda e: e.tensor_tensor(gs[:, 1:2], ap[:, 1:2], ap[:, 3:4],
                                             op=OP.add))
    for r in range(2, 8):
        k.op("vector", lambda e, r=r: e.tensor_tensor(
            gs[:, 0:1], gs[:, 0:1], ap[:, 2 * r:2 * r + 1], op=OP.add))
        k.op("vector", lambda e, r=r: e.tensor_tensor(
            gs[:, 1:2], gs[:, 1:2], ap[:, 2 * r + 1:2 * r + 2], op=OP.add))
    t_s0 = k.op("vector", lambda e: e.tensor_copy(gs[:, 2:3], gs[:, 0:1]), inc=True)
    i1, ps1, f1 = pool.acquire()
    t_t1 = k.op("tensor", lambda e: e.transpose(ps1[0:2, 0:128], gs[:, 0:2],
                                                cs.idf[:, :]),
                deps=[t_s0, f1, cs.tok], inc=True)
    t_c1 = k.op("vector", lambda e: e.tensor_copy(g2[0:2, 0:128], ps1[0:2, 0:128]),
                deps=[t_t1], inc=True)
    pool.release(i1, t_c1)
    t_r1 = k.op("vector", lambda e: e.tensor_reduce(g2[0:2, 128:129],
                                                    g2[0:2, 0:128], axis=AX.X,
                                                    op=OP.add), inc=True)
    i2, ps2, f2 = pool.acquire()
    t_t2 = k.op("tensor", lambda e: e.transpose(ps2[0:1, 0:2], g2[0:2, 128:129],
                                                cs.idf[0:2, 0:2]),
                deps=[t_r1, f2], inc=True)
    t_c2 = k.op("vector", lambda e: e.tensor_copy(g2[0:1, 0:2], ps2[0:1, 0:2]),
                deps=[t_t2], inc=True)
    pool.release(i2, t_c2)
    i3, ps3, f3 = pool.acquire()
    t_t3 = k.op("tensor", lambda e: e.matmul(ps3[0:128, 0:2], cs.or_[0:1, :],
                                             g2[0:1, 0:2], start=True, stop=True),
                deps=[t_c2, f3], inc=True)
    t_bc = k.op("vector", lambda e: e.tensor_copy(gs[:, 0:2], ps3[0:128, 0:2]),
                deps=[t_t3], inc=True)
    pool.release(i3, t_bc)

    k.op("vector", lambda e: e.tensor_scalar_mul(gs[:, 2:3], gs[:, 0:1], 1.0 / N),
         deps=[t_bc])
    k.op("vector", lambda e: e.tensor_scalar_mul(gs[:, 3:4], gs[:, 1:2], 1.0 / N))
    k.op("vector", lambda e: e.tensor_tensor(gs[:, 4:5], gs[:, 2:3], gs[:, 2:3],
                                             op=OP.mult))
    k.op("vector", lambda e: e.tensor_tensor(gs[:, 3:4], gs[:, 3:4], gs[:, 4:5],
                                             op=OP.subtract))
    t_s = k.op("vector", lambda e: e.tensor_scalar_add(gs[:, 3:4], gs[:, 3:4], 1e-5),
               inc=True)
    t_l = k.op("scalar", lambda e: e.activation(out=gs[:, 3:4], in_=gs[:, 3:4],
                                                func=AF.Ln), deps=[t_s], inc=True)
    t_r = k.op("scalar", lambda e: e.activation(out=gs[:, 3:4], in_=gs[:, 3:4],
                                                func=AF.Exp, scale=-0.5),
               deps=[t_l], inc=True)
    t_n = k.op("vector", lambda e: (e.tensor_scalar_mul(gs[:, 5:6], gs[:, 2:3], -1.0)),
               deps=[t_r], inc=True)

    xc = FM(k, "xc", DM, Tc, F32)
    cc = FM(k, "cc", DM, Tc, F32)
    rb = FM(k, "rb", DM, Tc, BF16)
    stats = k.sbuf("st", [1, 3 * Tc], F32)
    bias2 = k.sbuf("bi", [128, 2 * Tc], F32)
    guard = ()
    r_last = None
    st_tok = None
    for ci, c0 in enumerate(range(0, T, Tc)):
        txc = load_fm(k, xc, x_h, src=(c0, c0 + Tc),
                      deps=[r_last] if r_last else ())
        tcc = load_fm(k, cc, c_h, src=(c0, c0 + Tc),
                      deps=[r_last] if r_last else ())
        r_toks = []
        for j in range(8):
            k.op("vector", lambda e, j=j: e.tensor_scalar(
                cc.t(j), cc.t(j), gs[:, 5:6], gs[:, 3:4], op0=OP.add, op1=OP.mult),
                deps=[tcc, t_n, t_r] if j == 0 else ())
            t2 = k.op("vector", lambda e, j=j: e.tensor_add(
                cc.t(j), xc.t(j), cc.t(j)), deps=[txc] if j == 0 else (), inc=True)
            r_toks.append(t2)
        r_last = r_toks[-1]
        tok, _, guard = ln_chunk(k, pool, cs, r_fm=cc, sq_bf=rb, rb_bf=rb,
                                 y_fm=cc, yb_fm=None, stats_sb=stats,
                                 bias_sb=bias2, r_toks=r_toks, n=Tc, guard=guard)
        st_tok = store_fm(k, cc, y_h, dst=(c0, c0 + Tc), deps=[tok], eng="sync")
    k.op("sync", lambda e: e.nop(), deps=[k.fence("sync", dma=True)])


def l0c_inputs(xslice, cslice, allparts):
    return {
        "x": np.ascontiguousarray(xslice),
        "cin": np.ascontiguousarray(cslice),
        "allparts": np.ascontiguousarray(allparts),
        **const_inputs(),
    }


def _build_l3a(k, full):
    T, Tc, Lc = TCORE, 512, 64
    NCH = Tc // Lc
    cs = Consts(k)
    pool = PsumPool(k, 4, "pp")
    stage = StagePool(k, "stg", 8, F32)
    x_h = k.dram_in("x", [DM, T], F32)
    wk = StreamW(k, "wk", k.dram_in("Wk", [DM, DM], F32), DM, DM, stage)
    wg = StreamW(k, "wg", k.dram_in("Wg", [DM, DM], F32), DM, DM, stage)
    wv_h = k.dram_in("Wv", [DM, DM], F32)
    cmask_h = k.dram_in("cmask", [128, Tc], F32)
    tri_h = k.dram_in("trimask", [64, 64], BF16)
    cmask = k.sbuf("cmaskb", [128, Tc], F32)
    tri = k.sbuf("trib", [64, 64], BF16)
    k.dma("sync", cmask[:], cmask_h)
    t_cm = k.dma("sync", tri[:], tri_h)
    if full:
        wq = StreamW(k, "wq", k.dram_in("Wq", [DM, DM], F32), DM, DM, stage)
        wo = StreamW(k, "wo", k.dram_in("Wo", [DM, DM], F32), DM, DM, stage)
        pS_h = k.dram_in("pred_S", [128, 1024 * 7], F32)
        pgs_h = k.dram_in("pred_gsum", [128, 8 * 7], F32)
        y_h = k.dram_out("y", [DM, T], F32)
    else:
        S_h = k.dram_out("S", [128, 1024], F32)
        gs_h = k.dram_out("gsum", [128, 8], F32)

    wv = k.sbuf("wvsb", [128, 8 * DM], BF16)

    def stage_load(dst_sb, hbm, P, F, deps=()):
        si, raw, sfree = stage.acquire()
        t = k.dma("sync", raw[0:P, 0:F], hbm, deps=([sfree] if sfree else []) + list(deps))
        tc = k.op("gpsimd", lambda e: e.tensor_copy(dst_sb[0:P, 0:F], raw[0:P, 0:F]),
                  deps=[t], inc=True)
        stage.release(si, tc)
        return tc
    t_wv = None
    for kc in range(8):
        t_wv = stage_load(_FMSlice(wv, kc * DM, 128),
                          wv_h[kc * 128:(kc + 1) * 128, :], 128, DM)

    xb = FM(k, "xb", DM, T, BF16)
    xcl = FM(k, "xcl", DM, Tc, F32)
    tx = None
    for c0 in range(0, T, Tc):
        t = load_fm(k, xcl, x_h, src=(c0, c0 + Tc), deps=[tx] if tx else ())
        for j in range(8):
            tx = k.op("vector", lambda e, j=j, c0=c0: e.tensor_copy(
                xb.t(j, c0, c0 + Tc), xcl.t(j, 0, Tc)), deps=[t] if j == 0 else (),
                inc=(j == 7))
    txb = tx

    S = k.sbuf("Ssb", [128, 8 * 128], F32)
    Sbf = k.sbuf("Sbf", [128, 8 * 128], BF16)
    gsum = k.sbuf("gsum2", [128, 8], F32)
    tzS = k.op("vector", lambda e: e.memset(S[:], 0.0), inc=True)
    k.op("vector", lambda e: e.memset(gsum[:], 0.0))
    if full:
        pgs = k.sbuf("pgs", [128, 8 * 7], F32)
        psA = k.sbuf("psA", [128, 1024], F32)
        darg = k.sbuf("dargS", [128, 1024], F32)
        t_pg = k.dma("sync", pgs[:], pgs_h)
        t_S0 = tzS
        for r in range(7):
            t_pS = k.dma("sync", psA[:], pS_h[:, r * 1024:(r + 1) * 1024],
                         deps=[t_S0])
            gsr = pgs[:, r * 8:(r + 1) * 8]
            rep = bass.AP(gsr.tensor, gsr.offset,
                          [list(gsr.ap[0]), list(gsr.ap[1]), [0, 128]])
            k.op("vector", lambda e, rep=rep: e.tensor_copy(
                darg.rearrange("p (h v) -> p h v", h=8), rep),
                deps=[t_pg, tzS])
            t_e = k.op("scalar", lambda e: e.activation(
                out=darg[:], in_=darg[:], func=AF.Exp),
                deps=[k.fence("vector")], inc=True)
            k.op("vector", lambda e: e.tensor_tensor(S[:], S[:], darg[:],
                                                     op=OP.mult), deps=[t_e])
            t_S0 = k.op("vector", lambda e: e.tensor_tensor(
                S[:], S[:], psA[:], op=OP.add), deps=[t_pS], inc=True)
    else:
        t_S0 = tzS
    t_Sb = None
    for j in range(8):
        t_Sb = k.op("vector", lambda e, j=j: e.tensor_copy(
            Sbf[:, j * 128:(j + 1) * 128], S[:, j * 128:(j + 1) * 128]),
            deps=[t_S0] if j == 0 else (), inc=(j == 7))

    gB = FM(k, "gB", DM, Tc, F32)
    kf = FM(k, "kf", DM, Tc, F32)
    kbar = FM(k, "kbar", DM, Tc, BF16)
    qtl = FM(k, "qtl", DM, Tc, BF16) if full else None
    v_tm = k.sbuf("vtm", [64, 8 * DM], BF16)
    egend = k.sbuf("egend", [128, 8 * NCH], F32)
    egendK = k.sbuf("egendK", [128, 8 * NCH], F32)
    o_c = FM(k, "oc", DM, Tc, BF16) if full else None
    ku = k.sbuf("ku", [128, Lc], BF16)
    kuT = k.sbuf("kuT", [64, 128], BF16)
    pT = k.psum("pT", [64, 128], BF16)
    pT_free = [None]
    scr = k.sbuf("scr", [128, Tc], F32)
    scrb1 = k.sbuf("scrb1", [128, Tc], BF16)
    scrb2 = k.sbuf("scrb2", [128, Tc], BF16)
    rows = k.sbuf("rows", [1, 2 * Tc], F32)
    bias_b = k.sbuf("biasb", [128, Tc], F32)
    s_c = kf if full else None
    rb = qtl if full else None
    stats = k.sbuf("st", [1, 3 * Tc], F32)
    bias2 = k.sbuf("bi", [128, 2 * Tc], F32)
    zcol = k.sbuf("zcol", [128, 1], F32)
    c55 = k.sbuf("c55", [128, 2], F32)
    k.op("vector", lambda e: e.memset(c55[:, 0:1], EXPC))
    tc55 = k.op("vector", lambda e: e.memset(c55[:, 1:2], -EXPC), inc=True)
    tzc = k.op("vector", lambda e: e.memset(zcol[:], 0.0), inc=True)

    guard = ()
    st_tok = None
    t_tail = t_Sb
    for ci, c0 in enumerate(range(0, T, Tc)):
        t_sp = None
        for fo in range(8):
            lhsT, ext = wg.prep(k, fo)
            i, ps, ftok = pool.acquire()
            mm = None
            for kc in range(8):
                mm = k.op("tensor", lambda e, kc=kc, fo=fo, ps=ps, c0=c0, lhsT=lhsT:
                          e.matmul(ps[0:128, 0:Tc], lhsT(kc, fo),
                                   xb.t(kc, c0, c0 + Tc), start=(kc == 0),
                                   stop=(kc == 7)),
                          deps=([txb, t_tail, ftok] + ext) if kc == 0 else (),
                          inc=(kc == 7))
            wg.done(k, fo, mm)
            t1 = k.op("scalar", lambda e, ps=ps: e.activation(
                out=scr[:, 0:Tc], in_=ps[0:128, 0:Tc], func=AF.Abs),
                deps=[mm, t_sp], inc=True)
            t2 = k.op("scalar", lambda e: e.activation(
                out=scr[:, 0:Tc], in_=scr[:, 0:Tc], func=AF.Exp, scale=-1.0),
                deps=[t1], inc=True)
            t3 = k.op("scalar", lambda e: e.activation(
                out=scr[:, 0:Tc], in_=scr[:, 0:Tc], func=AF.Ln, bias=1.0),
                deps=[t2], inc=True)
            t4 = k.op("vector", lambda e, ps=ps, fo=fo: e.tensor_scalar(
                gB.t(fo), ps[0:128, 0:Tc], 0.0, None, op0=OP.min),
                deps=[t3, mm], inc=True)
            t_sp = k.op("vector", lambda e, fo=fo: e.tensor_tensor(
                gB.t(fo), gB.t(fo), scr[:, 0:Tc], op=OP.subtract), inc=True)
            pool.release(i, t_sp)
        t_G = None
        for j in range(8):
            k.op("vector", lambda e, j=j: e.tensor_reduce(
                bias_b[:, 0:1], gB.t(j), axis=AX.X,
                op=OP.add), deps=[t_sp] if j == 0 else ())
            k.op("vector", lambda e, j=j: e.tensor_tensor(
                gsum[:, j:j + 1], gsum[:, j:j + 1], bias_b[:, 0:1], op=OP.add))
            t_G = k.op("vector", lambda e, j=j: e.tensor_tensor_scan(
                gB.t(j), cmask[:, 0:Tc], gB.t(j), zcol[:],
                op0=OP.mult, op1=OP.add), deps=[tzc], inc=True)
        t_eg = None
        for j in range(8):
            base = gB.t(j)
            sv = bass.AP(base.tensor, base.offset + (Lc - 1),
                         [list(base.ap[0]), [Lc, NCH]])
            k.op("scalar", lambda e, j=j, sv=sv: e.activation(
                out=egend[:, j * NCH:(j + 1) * NCH], in_=sv, func=AF.Exp),
                deps=[t_G], inc=False)
            t_eg = k.op("scalar", lambda e, j=j, sv=sv: e.activation(
                out=egendK[:, j * NCH:(j + 1) * NCH], in_=sv, func=AF.Exp,
                bias=c55[:, 0:1]), deps=[t_G, tc55], inc=True)
        tv_toks = []
        for cc in range(NCH):
            for vh in range(2):
                i, ps, ftok = pool.acquire()
                mm = None
                for kc in range(8):
                    mm = k.op("tensor", lambda e, kc=kc, cc=cc, vh=vh, ps=ps, c0=c0:
                              e.matmul(ps[0:64, 0:512],
                                       xb.t(kc, c0 + cc * Lc, c0 + (cc + 1) * Lc),
                                       wv[:, kc * DM + vh * 512:kc * DM + (vh + 1) * 512],
                                       start=(kc == 0), stop=(kc == 7)),
                              deps=[txb, t_wv, ftok, st_tok] if kc == 0 else (),
                              inc=(kc == 7))
                tev = k.op("vector", lambda e, cc=cc, vh=vh, ps=ps: e.tensor_copy(
                    v_tm[0:64, cc * DM + vh * 512:cc * DM + (vh + 1) * 512],
                    ps[0:64, 0:512]), deps=[mm], inc=True)
                pool.release(i, tev)
                tv_toks.append(tev)
        def proj_norm(w, dst_bf, scale_exp_sign, deps0):
            t_p = matmul_fm(k, pool, w=w, nK=8, nF=8, T=Tc,
                            rhs=lambda kc, a0, a1, c0=c0: xb.t(kc, c0 + a0, c0 + a1),
                            evac=lambda e, ps, fo, a0, a1: e.activation(
                                out=kf.t(fo, a0, a1), in_=ps, func=AF.Copy),
                            evac_eng="scalar", mm_deps=[txb] + deps0)
            toks = []
            for j in range(8):
                tsq = k.op("scalar", lambda e, j=j: e.activation(
                    out=scrb1[:, 0:Tc], in_=kf.t(j), func=AF.Square),
                    deps=[t_p, toks[-1] if toks else None], inc=True)
                i1, ps1, f1 = pool.acquire()
                mm = k.op("tensor", lambda e, ps=ps1: e.matmul(
                    ps[0:1, 0:Tc], cs.ocb[:, 0:1], scrb1[:, 0:Tc],
                    start=True, stop=True), deps=[tsq, f1, cs.tok], inc=True)
                t1 = k.op("scalar", lambda e, ps=ps1: e.activation(
                    out=rows[0:1, 0:Tc], in_=ps[0:1, 0:Tc], func=AF.Sqrt),
                    deps=[mm], inc=True)
                k.op("vector", lambda e: e.tensor_scalar_max(
                    rows[0:1, 0:Tc], rows[0:1, 0:Tc], 1e-12), deps=[t1])
                t2 = k.op("vector", lambda e: e.reciprocal(
                    rows[0:1, Tc:2 * Tc], rows[0:1, 0:Tc]), inc=True)
                pool.release(i1, t2)
                tb = bcast_chunk(k, pool, cs, bias_b[:, 0:Tc],
                                 rows[0:1, Tc:2 * Tc], 128, [t2], n=Tc)
                if scale_exp_sign < 0:
                    te = k.op("scalar", lambda e, j=j: e.activation(
                        out=scrb1[:, 0:Tc], in_=gB.t(j), func=AF.Exp,
                        scale=-1.0, bias=c55[:, 1:2]), deps=[t_G, tb, tc55],
                        inc=True)
                else:
                    te = k.op("scalar", lambda e, j=j: e.activation(
                        out=scrb1[:, 0:Tc], in_=gB.t(j), func=AF.Exp,
                        scale=1.0), deps=[t_G, tb], inc=True)
                k.op("vector", lambda e, j=j: e.tensor_tensor(
                    scrb2[:, 0:Tc], kf.t(j), bias_b[:, 0:Tc], op=OP.mult),
                    deps=[tb, te])
                t3 = k.op("vector", lambda e, j=j, dst_bf=dst_bf: e.tensor_tensor(
                    dst_bf.t(j), scrb2[:, 0:Tc], scrb1[:, 0:Tc], op=OP.mult),
                    inc=True)
                toks.append(t3)
            return toks
        tk = proj_norm(wk, kbar, -1.0, [t_tail] + ([st_tok] if st_tok else []))
        tq = (proj_norm(wq, qtl, +1.0, [tk[-1]] + list(guard))
              if full else [])
        t_state = t_Sb if ci == 0 else t_state
        for cc in range(NCH):
            for h in range(8):
                ecol = egend[:, h * NCH + cc:h * NCH + cc + 1]
                iA, psAm, fA = pool.acquire()
                mmA = k.op("tensor", lambda e, h=h, cc=cc, ps=psAm: e.matmul(
                    ps[0:64, 0:Lc], kbar.t(h, cc * Lc, (cc + 1) * Lc),
                    qtl.t(h, cc * Lc, (cc + 1) * Lc), start=True, stop=True),
                    deps=[tk[h], tq[h] if full else None, fA, t_cm], inc=True) if full else None
                if full:
                    tA = k.op("vector", lambda e, ps=psAm: e.tensor_tensor(
                        scrb1[0:64, 0:Lc], ps[0:64, 0:Lc], tri[:, :], op=OP.mult),
                        deps=[mmA], inc=True)
                    pool.release(iA, tA)
                    iO, psO, fO = pool.acquire()
                    mmO = k.op("tensor", lambda e, h=h, cc=cc, ps=psO:
                               e.matmul(ps[0:128, 0:Lc],
                                        v_tm[0:64, cc * DM + h * 128:cc * DM + (h + 1) * 128],
                                        scrb1[0:64, 0:Lc], start=True, stop=False),
                               deps=[tA, tv_toks[-1], fO], inc=False)
                    mmO = k.op("tensor", lambda e, h=h, cc=cc, ps=psO: e.matmul(
                        ps[0:128, 0:Lc], Sbf[:, h * 128:(h + 1) * 128],
                        qtl.t(h, cc * Lc, (cc + 1) * Lc), start=False, stop=True),
                        deps=[t_state], inc=True)
                    tO = k.op("scalar", lambda e, h=h, cc=cc, ps=psO: e.activation(
                        out=o_c.t(h, cc * Lc, (cc + 1) * Lc), in_=ps[0:128, 0:Lc],
                        func=AF.Copy), deps=[mmO], inc=True)
                    pool.release(iO, tO)
                ecolK = egendK[:, h * NCH + cc:h * NCH + cc + 1]
                t_ku = k.op("vector", lambda e, h=h, cc=cc, ecolK=ecolK: e.tensor_scalar_mul(
                    ku[:, 0:Lc], kbar.t(h, cc * Lc, (cc + 1) * Lc), ecolK),
                    deps=[tk[h], t_eg], inc=True)
                mmT = k.op("tensor", lambda e: e.transpose(
                    pT[0:Lc, 0:128], ku[:, 0:Lc], cs.id[:, :]),
                    deps=[t_ku, pT_free[0]], inc=True)
                tTe = k.op("vector", lambda e: e.tensor_copy(
                    kuT[0:Lc, 0:128], pT[0:Lc, 0:128]), deps=[mmT], inc=True)
                pT_free[0] = tTe
                iS, psS, fS = pool.acquire()
                mmS = k.op("tensor", lambda e, h=h, cc=cc, ps=psS: e.matmul(
                    ps[0:128, 0:128], kuT[0:Lc, 0:128],
                    v_tm[0:64, cc * DM + h * 128:cc * DM + (h + 1) * 128],
                    start=True, stop=True), deps=[tTe, tv_toks[-1], fS], inc=True)
                k.op("vector", lambda e, h=h, ecol=ecol: e.tensor_scalar_mul(
                    S[:, h * 128:(h + 1) * 128], S[:, h * 128:(h + 1) * 128], ecol),
                    deps=[mmO if full else mmS, t_eg])
                t_up = k.op("vector", lambda e, h=h, ps=psS: e.tensor_tensor(
                    S[:, h * 128:(h + 1) * 128], S[:, h * 128:(h + 1) * 128],
                    ps[0:128, 0:128], op=OP.add), deps=[mmS], inc=True)
                pool.release(iS, t_up)
                t_state = k.op("vector", lambda e, h=h: e.tensor_copy(
                    Sbf[:, h * 128:(h + 1) * 128], S[:, h * 128:(h + 1) * 128]),
                    inc=True)
        t_tail = t_state
        if full:
            t_o = matmul_fm(k, pool, w=wo, nK=8, nF=8, T=Tc,
                            rhs=lambda kc, a0, a1: o_c.t(kc, a0, a1),
                            evac=lambda e, ps, fo, a0, a1: e.activation(
                                out=s_c.t(fo, a0, a1), in_=ps, func=AF.Copy),
                            evac_eng="scalar",
                            mm_deps=[k.fence("scalar")] + ([st_tok] if st_tok else []))
            txc2 = load_fm(k, xcl, x_h, src=(c0, c0 + Tc), deps=[t_o, txb])
            r_toks = [k.op("vector", lambda e, j=j: e.tensor_add(
                s_c.t(j), xcl.t(j), s_c.t(j)),
                deps=[txc2, t_o] if j == 0 else (), inc=True) for j in range(8)]
            tok, _, guard = ln_chunk(k, pool, cs, r_fm=s_c, sq_bf=rb, rb_bf=rb,
                                     y_fm=s_c, yb_fm=None, stats_sb=stats,
                                     bias_sb=bias2, r_toks=r_toks, n=Tc,
                                     guard=guard)
            st_tok = store_fm(k, s_c, y_h, dst=(c0, c0 + Tc), deps=[tok],
                              eng="sync")
    if not full:
        k.dma("sync", S_h, S[:], deps=[t_tail])
        k.dma("sync", gs_h, gsum[:], deps=[k.fence("vector")])
    k.op("sync", lambda e: e.nop(), deps=[k.fence("sync", dma=True)])


def build_l3a_a(k):
    _build_l3a(k, full=False)


def build_l3a_b(k):
    _build_l3a(k, full=True)


def l3a_inputs(xslice, d, full, pred_S=None, pred_gsum=None):
    cmask = np.ones((128, 512), np.float32)
    cmask[:, 0::64] = 0.0
    tri = np.triu(np.ones((64, 64), np.float32)) * np.exp(55.0).astype(np.float32)
    m = {
        "x": np.ascontiguousarray(xslice),
        "Wk": np.ascontiguousarray(d["L3_attn_Wk"]),
        "Wg": np.ascontiguousarray(d["L3_attn_Wg"]),
        "Wv": np.ascontiguousarray(d["L3_attn_Wv"]),
        "cmask": cmask,
        "trimask": np.ascontiguousarray(tri).astype(bf16()),
        **const_inputs(),
    }
    if full:
        m.update({
            "Wq": np.ascontiguousarray(d["L3_attn_Wq"]),
            "Wo": np.ascontiguousarray(d["L3_attn_Wo"]),
            "pred_S": np.ascontiguousarray(pred_S),
            "pred_gsum": np.ascontiguousarray(pred_gsum),
        })
    return m


PI32 = 3.14159274101257324
PIA = 3.140625
PIB = float(np.float32(np.pi - 3.140625))


def build_l1a0(k):
    T, Tc = TCORE, 512
    cs = Consts(k)
    pool = PsumPool(k, 4, "pp")
    stage = StagePool(k, "stg", 8, F32)
    x_h = k.dram_in("x", [DM, T], F32)
    wk = StreamW(k, "wk", k.dram_in("wk", [DM, DM], F32), DM, DM, stage)
    ss_h = k.dram_out("ssum", [128, 8], F32)
    pp_h = k.dram_out("parprod", [128, 8], F32)
    xb = FM(k, "xb", DM, T, BF16)
    xcl = FM(k, "xcl", DM, Tc, F32)
    tx = None
    for c0 in range(0, T, Tc):
        t = load_fm(k, xcl, x_h, src=(c0, c0 + Tc), deps=[tx] if tx else ())
        for j in range(8):
            tx = k.op("vector", lambda e, j=j, c0=c0: e.tensor_copy(
                xb.t(j, c0, c0 + Tc), xcl.t(j, 0, Tc)), deps=[t] if j == 0 else (),
                inc=(j == 7))
    txb = tx
    kf = FM(k, "kf", DM, Tc, F32)
    sgn = k.sbuf("sgn", [128, Tc], F32)
    ppr = k.sbuf("ppr", [128, Tc], F32)
    ss = k.sbuf("ss", [128, 8], F32)
    pp = k.sbuf("pp", [128, 8], F32)
    red = k.sbuf("red", [128, 1], F32)
    zcol = k.sbuf("zcol", [128, 1], F32)
    ocol = k.sbuf("ocol", [128, 1], F32)
    k.op("vector", lambda e: e.memset(ss[:], 0.0))
    k.op("vector", lambda e: e.memset(pp[:], 1.0))
    k.op("vector", lambda e: e.memset(ocol[:], 1.0))
    tz = k.op("vector", lambda e: e.memset(zcol[:], 0.0), inc=True)
    zrep = bass.AP(zcol.ap().tensor, zcol.ap().offset,
                   [list(zcol.ap().ap[0]), [0, Tc]])
    t_last = None
    for ci, c0 in enumerate(range(0, T, Tc)):
        t_p = matmul_fm(k, pool, w=wk, nK=8, nF=8, T=Tc,
                        rhs=lambda kc, a0, a1, c0=c0: xb.t(kc, c0 + a0, c0 + a1),
                        evac=lambda e, ps, fo, a0, a1: e.activation(
                            out=kf.t(fo, a0, a1), in_=ps, func=AF.Copy),
                        evac_eng="scalar", mm_deps=[txb, t_last])
        for j in range(8):
            t1 = k.op("scalar", lambda e, j=j: e.activation(
                out=sgn[:, 0:Tc], in_=kf.t(j), func=AF.Sign, scale=-1.0),
                deps=[t_p, t_last], inc=True)
            k.op("vector", lambda e, j=j: e.tensor_reduce(
                red[:], sgn[:, 0:Tc], axis=AX.X, op=OP.add), deps=[t1])
            k.op("vector", lambda e, j=j: e.tensor_tensor(
                ss[:, j:j + 1], ss[:, j:j + 1], red[:], op=OP.add))
            k.op("vector", lambda e: e.tensor_tensor(
                ppr[:, 0:Tc], sgn[:, 0:Tc], sgn[:, 0:Tc], op=OP.mult))
            k.op("vector", lambda e: e.tensor_scalar(
                ppr[:, 0:Tc], ppr[:, 0:Tc], -2.0, 1.0, op0=OP.mult, op1=OP.add))
            k.op("vector", lambda e: e.tensor_tensor_scan(
                ppr[:, 0:Tc], ppr[:, 0:Tc], zrep, ocol[:],
                op0=OP.mult, op1=OP.add), deps=[tz])
            t_last = k.op("vector", lambda e, j=j: e.tensor_tensor(
                pp[:, j:j + 1], pp[:, j:j + 1], ppr[:, Tc - 1:Tc], op=OP.mult),
                inc=True)
    k.dma("sync", ss_h, ss[:], deps=[t_last])
    k.dma("sync", pp_h, pp[:], deps=[t_last])
    k.op("sync", lambda e: e.nop(), deps=[k.fence("sync", dma=True)])


def l1a0_inputs(xslice, d):
    return {"x": np.ascontiguousarray(xslice),
            "wk": np.ascontiguousarray(d["L1_attn_wk"]),
            **const_inputs()}


def _build_l1a(k, full):
    T, Tc, Lc = TCORE, 512, 64
    NCH = Tc // Lc
    cs = Consts(k)
    pool = PsumPool(k, 4, "pp")
    stage = StagePool(k, "stg", 8, F32)
    x_h = k.dram_in("x", [DM, T], F32)
    wk = StreamW(k, "wk", k.dram_in("wk", [DM, DM], F32), DM, DM, stage)
    wv_h = k.dram_in("wv", [DM, DM], F32)
    cmask_h = k.dram_in("cmask", [128, Tc], F32)
    tri_h = k.dram_in("trimask", [64, 64], BF16)
    cmask = k.sbuf("cmaskb", [128, Tc], F32)
    tri = k.sbuf("trib", [64, 64], BF16)
    k.dma("sync", cmask[:], cmask_h)
    t_cm = k.dma("sync", tri[:], tri_h)
    pss_h = k.dram_in("pred_ssum", [128, 8 * 7], F32)
    ppp_h = k.dram_in("pred_par", [128, 8 * 7], F32)
    pss = k.sbuf("pss", [128, 56], F32)
    ppp = k.sbuf("ppp", [128, 56], F32)
    k.dma("sync", pss[:], pss_h)
    t_pload = k.dma("sync", ppp[:], ppp_h)
    if full:
        wq = StreamW(k, "wq", k.dram_in("wq", [DM, DM], F32), DM, DM, stage)
        wo = StreamW(k, "wo", k.dram_in("wo", [DM, DM], F32), DM, DM, stage)
        wg1_h = k.dram_in("wg1", [DM, 128], F32)
        wg2_h = k.dram_in("wg2", [128, DM], F32)
        nw_h = k.dram_in("norm_w", [128, 8], F32)
        pS_h = k.dram_in("pred_S", [128, 2048 * 7], F32)
        pgs_h = k.dram_in("pred_gsum", [128, 8 * 7], F32)
        y_h = k.dram_out("y", [DM, T], F32)
    else:
        S_h = k.dram_out("S", [128, 2048], F32)
        gs_h = k.dram_out("gsum", [128, 8], F32)

    wv = k.sbuf("wvsb", [128, 8 * DM], BF16)

    def stage_load(dst_sb, hbm, P, F, deps=()):
        si, raw, sfree = stage.acquire()
        t = k.dma("sync", raw[0:P, 0:F], hbm,
                  deps=([sfree] if sfree else []) + list(deps))
        tc = k.op("gpsimd", lambda e: e.tensor_copy(dst_sb[0:P, 0:F], raw[0:P, 0:F]),
                  deps=[t], inc=True)
        stage.release(si, tc)
        return tc
    t_wv = None
    for kc in range(8):
        t_wv = stage_load(_FMSlice(wv, kc * DM, 128),
                          wv_h[kc * 128:(kc + 1) * 128, :], 128, DM)
    if full:
        wg1 = k.sbuf("wg1sb", [128, 8 * 128], BF16)
        wg2 = k.sbuf("wg2sb", [128, DM], BF16)
        nw = k.sbuf("nwsb", [128, 8], F32)
        t_g1 = stage_load(wg1, wg1_h.rearrange("(n p) f -> p n f", p=128), 128, 8 * 128)
        t_g2 = stage_load(wg2, wg2_h, 128, DM)
        si, raw, sfree = stage.acquire()
        t = k.dma("sync", raw[0:128, 0:8], nw_h, deps=[sfree] if sfree else ())
        t_nw = k.op("gpsimd", lambda e: e.tensor_copy(nw[:, :], raw[0:128, 0:8]),
                    deps=[t], inc=True)
        stage.release(si, t_nw)

    xb = FM(k, "xbc", DM, Tc, BF16)
    xcl = FM(k, "xcl", DM, Tc, F32)

    koff = k.sbuf("koff", [128, 8], F32)
    poff = k.sbuf("poff", [128, 8], F32)
    k.op("vector", lambda e: e.memset(koff[:], 0.0), deps=[t_pload])
    k.op("vector", lambda e: e.memset(poff[:], 1.0))
    for r in range(7):
        k.op("vector", lambda e, r=r: e.tensor_tensor(
            koff[:], koff[:], pss[:, r * 8:(r + 1) * 8], op=OP.add))
        k.op("vector", lambda e, r=r: e.tensor_tensor(
            poff[:], poff[:], ppp[:, r * 8:(r + 1) * 8], op=OP.mult))
    t_off = k.op("vector", lambda e: e.tensor_copy(koff[:], koff[:]), inc=True)

    S = k.sbuf("Ssb", [128, 2048], F32)
    Sbf = k.sbuf("Sbf", [128, 2048], BF16)
    gsum = k.sbuf("gsum2", [128, 8], F32)
    gB = FM(k, "gB", DM, Tc, F32)
    kf = FM(k, "kf", DM, Tc, F32)
    psA = gB.sb[:, 0:2048]
    darg = kf.sb[:, 0:2048]
    tzS = k.op("vector", lambda e: e.memset(S[:], 0.0), inc=True)
    k.op("vector", lambda e: e.memset(gsum[:], 0.0))
    if full:
        pgs = k.sbuf("pgs", [128, 56], F32)
        t_pg = k.dma("sync", pgs[:], pgs_h)
        t_S0 = tzS
        for r in range(7):
            t_pS = k.dma("sync", psA[:], pS_h[:, r * 2048:(r + 1) * 2048],
                         deps=[t_S0])
            gsr = pgs[:, r * 8:(r + 1) * 8]
            rep = bass.AP(gsr.tensor, gsr.offset,
                          [list(gsr.ap[0]), list(gsr.ap[1]), [0, 2], [0, 128]])
            k.op("vector", lambda e, rep=rep: e.tensor_copy(
                darg.rearrange("p (h two v) -> p h two v", h=8, two=2), rep),
                deps=[t_pg, tzS])
            t_e = k.op("scalar", lambda e: e.activation(
                out=darg[:], in_=darg[:], func=AF.Exp),
                deps=[k.fence("vector")], inc=True)
            k.op("vector", lambda e: e.tensor_tensor(S[:], S[:], darg[:],
                                                     op=OP.mult), deps=[t_e])
            t_S0 = k.op("vector", lambda e: e.tensor_tensor(
                S[:], S[:], psA[:], op=OP.add), deps=[t_pS], inc=True)
    else:
        t_S0 = tzS
    t_Sb = None
    for j in range(16):
        t_Sb = k.op("vector", lambda e, j=j: e.tensor_copy(
            Sbf[:, j * 128:(j + 1) * 128], S[:, j * 128:(j + 1) * 128]),
            deps=[t_S0] if j == 0 else (), inc=(j == 15))

    qr = FM(k, "qr", 2048, Tc, BF16) if full else None
    kr = FM(k, "kr", 2048, Tc, BF16)
    v_tm = k.sbuf("vtm", [64, 8 * DM], BF16)
    egend = k.sbuf("egend", [128, 8 * NCH], F32)
    egendK = k.sbuf("egendK", [128, 8 * NCH], F32)
    o_c = FM(k, "oc", DM, Tc, BF16) if full else None
    gate_c = FM(k, "gt", DM, Tc, BF16) if full else None
    ku = k.sbuf("ku", [128, Lc], BF16)
    Ab = k.sbuf("Ab", [64, 64], BF16)
    kuT = k.sbuf("kuT", [64, 128], BF16)
    pT = k.psum("pT", [64, 128], BF16)
    pT_free = [None]
    scr = k.sbuf("scr", [128, Tc], F32)
    scr2 = k.sbuf("scr2", [128, Tc], F32)
    sgn = k.sbuf("sgn", [128, Tc], F32)
    par = k.sbuf("par", [128, Tc], F32)
    kint = k.sbuf("kint", [128, Tc], F32)
    kcar = k.sbuf("kcar", [128, 8], F32)
    pcar = k.sbuf("pcar", [128, 8], F32)
    zcol = k.sbuf("zcol", [128, 1], F32)
    c55 = k.sbuf("c55", [128, 2], F32)
    k.op("vector", lambda e: e.memset(c55[:, 0:1], EXPC))
    tc55 = k.op("vector", lambda e: e.memset(c55[:, 1:2], -EXPC), inc=True)
    tzc = k.op("vector", lambda e: e.memset(zcol[:], 0.0), inc=True)
    t_car = k.op("vector", lambda e: (e.tensor_copy(kcar[:], koff[:])),
                 deps=[t_off], inc=True)
    t_car = k.op("vector", lambda e: (e.tensor_copy(pcar[:], poff[:])), inc=True)
    zrep = bass.AP(zcol.ap().tensor, zcol.ap().offset,
                   [list(zcol.ap().ap[0]), [0, Tc]])
    ocol = k.sbuf("ocol", [128, 1], F32)
    k.op("vector", lambda e: e.memset(ocol[:], 1.0))
    orep = bass.AP(ocol.ap().tensor, ocol.ap().offset,
                   [list(ocol.ap().ap[0]), [0, Tc]])
    s_c = kf if full else None
    rb = qr if full else None
    gmid = k.sbuf("gmid", [128, Tc], BF16) if full else None
    stats = k.sbuf("st", [1, 3 * Tc], F32)
    bias2 = k.sbuf("bi", [128, 2 * Tc], F32)
    rows = k.sbuf("rows", [1, 2 * Tc], F32)

    guard = ()
    st_tok = None
    t_tail = t_Sb
    t_state = t_Sb
    for ci, c0 in enumerate(range(0, T, Tc)):
        t_xl = load_fm(k, xcl, x_h, src=(c0, c0 + Tc),
                       deps=[k.fence("tensor"), st_tok])
        txb = None
        for j in range(8):
            txb = k.op("vector", lambda e, j=j: e.tensor_copy(
                xb.t(j), xcl.t(j, 0, Tc)), deps=[t_xl] if j == 0 else (),
                inc=(j == 7))
        t_zk = matmul_fm(k, pool, w=wk, nK=8, nF=8, T=Tc,
                         rhs=lambda kc, a0, a1: xb.t(kc, a0, a1),
                         evac=lambda e, ps, fo, a0, a1: e.activation(
                             out=kf.t(fo, a0, a1), in_=ps, func=AF.Copy),
                         evac_eng="scalar", mm_deps=[txb, t_tail, st_tok])
        t_sp = None
        for fo in range(8):
            t1 = k.op("scalar", lambda e, fo=fo: e.activation(
                out=scr[:, 0:Tc], in_=kf.t(fo), func=AF.Abs),
                deps=[t_zk, t_sp], inc=True)
            t2 = k.op("scalar", lambda e: e.activation(
                out=scr[:, 0:Tc], in_=scr[:, 0:Tc], func=AF.Exp, scale=-1.0),
                deps=[t1], inc=True)
            t3 = k.op("scalar", lambda e: e.activation(
                out=scr[:, 0:Tc], in_=scr[:, 0:Tc], func=AF.Ln, bias=1.0),
                deps=[t2], inc=True)
            t4 = k.op("vector", lambda e, fo=fo: e.tensor_scalar(
                gB.t(fo), kf.t(fo), 0.0, None, op0=OP.max), deps=[t3], inc=True)
            t_sp = k.op("vector", lambda e, fo=fo: e.tensor_tensor(
                gB.t(fo), gB.t(fo), scr[:, 0:Tc], op=OP.add), inc=True)
            t_sp = k.op("vector", lambda e, fo=fo: e.tensor_scalar_mul(
                gB.t(fo), gB.t(fo), -1.0), inc=True)
        t_G = None
        for j in range(8):
            k.op("vector", lambda e, j=j: e.tensor_reduce(
                scr[:, 0:1], gB.t(j), axis=AX.X, op=OP.add),
                deps=[t_sp] if j == 0 else ())
            k.op("vector", lambda e, j=j: e.tensor_tensor(
                gsum[:, j:j + 1], gsum[:, j:j + 1], scr[:, 0:1], op=OP.add))
            t_G = k.op("vector", lambda e, j=j: e.tensor_tensor_scan(
                gB.t(j), cmask[:, 0:Tc], gB.t(j), zcol[:],
                op0=OP.mult, op1=OP.add), deps=[tzc], inc=True)
        t_eg = None
        for j in range(8):
            base = gB.t(j)
            sv = bass.AP(base.tensor, base.offset + (Lc - 1),
                         [list(base.ap[0]), [Lc, NCH]])
            k.op("scalar", lambda e, j=j, sv=sv: e.activation(
                out=egend[:, j * NCH:(j + 1) * NCH], in_=sv, func=AF.Exp),
                deps=[t_G], inc=False)
            t_eg = k.op("scalar", lambda e, j=j, sv=sv: e.activation(
                out=egendK[:, j * NCH:(j + 1) * NCH], in_=sv, func=AF.Exp,
                bias=c55[:, 0:1]), deps=[t_G, tc55], inc=True)
        t_q = None
        if full:
            t_q = matmul_fm(k, pool, w=wq, nK=8, nF=8, T=Tc,
                            rhs=lambda kc, a0, a1: xb.t(kc, a0, a1),
                            evac=lambda e, ps, fo, a0, a1: e.activation(
                                out=scr2[:, a0:a1] if False else gate_c.t(fo, a0, a1),
                                in_=ps, func=AF.Silu),
                            evac_eng="scalar", mm_deps=[txb, t_zk])
        t_th = None
        for j in range(8):
            t1 = k.op("scalar", lambda e, j=j: e.activation(
                out=sgn[:, 0:Tc], in_=kf.t(j), func=AF.Sign, scale=-1.0),
                deps=[t_zk, t_th, t_G], inc=True)
            k.op("vector", lambda e: e.tensor_tensor(
                par[:, 0:Tc], sgn[:, 0:Tc], sgn[:, 0:Tc], op=OP.mult), deps=[t1])
            k.op("vector", lambda e: e.tensor_scalar(
                par[:, 0:Tc], par[:, 0:Tc], -2.0, 1.0, op0=OP.mult, op1=OP.add))
            k.op("vector", lambda e, j=j: e.tensor_tensor_scan(
                par[:, 0:Tc], par[:, 0:Tc], zrep, pcar[:, j:j + 1],
                op0=OP.mult, op1=OP.add))
            k.op("vector", lambda e, j=j: e.tensor_copy(
                pcar[:, j:j + 1], par[:, Tc - 1:Tc]))
            k.op("vector", lambda e, j=j: e.tensor_tensor_scan(
                kint[:, 0:Tc], orep, sgn[:, 0:Tc], kcar[:, j:j + 1],
                op0=OP.mult, op1=OP.add))
            k.op("vector", lambda e, j=j: e.tensor_copy(
                kcar[:, j:j + 1], kint[:, Tc - 1:Tc]))
            k.op("vector", lambda e: e.tensor_scalar_mul(scr[:, 0:Tc], kint[:, 0:Tc], PI32))
            k.op("vector", lambda e: e.tensor_scalar_mul(scr2[:, 0:Tc], kint[:, 0:Tc], PIA))
            k.op("vector", lambda e: e.tensor_tensor(
                scr[:, 0:Tc], scr[:, 0:Tc], scr2[:, 0:Tc], op=OP.subtract))
            k.op("vector", lambda e: e.tensor_scalar_mul(scr2[:, 0:Tc], kint[:, 0:Tc], PIB))
            k.op("vector", lambda e: e.tensor_tensor(
                scr[:, 0:Tc], scr[:, 0:Tc], scr2[:, 0:Tc], op=OP.subtract))
            t_sin = k.op("vector", lambda e: e.tensor_tensor(
                scr2[:, 0:Tc], par[:, 0:Tc], scr[:, 0:Tc], op=OP.mult), inc=True)
            t_k = k.op("scalar", lambda e, j=j: e.activation(
                out=scr[:, 0:Tc], in_=kf.t(j), func=AF.Sigmoid),
                deps=[t_sin], inc=True)
            k.op("vector", lambda e, j=j: e.tensor_tensor(
                kr.t(2 * j), scr[:, 0:Tc], par[:, 0:Tc], op=OP.mult), deps=[t_k])
            t_th = k.op("vector", lambda e, j=j: e.tensor_tensor(
                kr.t(2 * j + 1), scr[:, 0:Tc], scr2[:, 0:Tc], op=OP.mult), inc=True)
            if full:
                k.op("vector", lambda e, j=j: e.tensor_scalar_mul(
                    scr[:, 0:Tc], gate_c.t(j), 0.0625), deps=[t_q])
                k.op("vector", lambda e, j=j: e.tensor_tensor(
                    qr.t(2 * j), scr[:, 0:Tc], par[:, 0:Tc], op=OP.mult))
                t_th = k.op("vector", lambda e, j=j: e.tensor_tensor(
                    qr.t(2 * j + 1), scr[:, 0:Tc], scr2[:, 0:Tc], op=OP.mult),
                    inc=True)
        t_dec = None
        for j in range(8):
            te = k.op("scalar", lambda e, j=j: e.activation(
                out=scr[:, 0:Tc], in_=gB.t(j), func=AF.Exp), deps=[t_G, t_th],
                inc=True)
            if full:
                k.op("vector", lambda e, j=j: e.tensor_tensor(
                    qr.t(2 * j), qr.t(2 * j), scr[:, 0:Tc], op=OP.mult), deps=[te])
                k.op("vector", lambda e, j=j: e.tensor_tensor(
                    qr.t(2 * j + 1), qr.t(2 * j + 1), scr[:, 0:Tc], op=OP.mult))
            te2 = k.op("scalar", lambda e, j=j: e.activation(
                out=scr2[:, 0:Tc], in_=gB.t(j), func=AF.Exp, scale=-1.0,
                bias=c55[:, 1:2]), deps=[tc55, k.fence("vector")], inc=True)
            k.op("vector", lambda e, j=j: e.tensor_tensor(
                kr.t(2 * j), kr.t(2 * j), scr2[:, 0:Tc], op=OP.mult), deps=[te2])
            t_dec = k.op("vector", lambda e, j=j: e.tensor_tensor(
                kr.t(2 * j + 1), kr.t(2 * j + 1), scr2[:, 0:Tc], op=OP.mult),
                inc=True)
        tv_toks = []
        for cc in range(NCH):
            for vh in range(2):
                i, ps, ftok = pool.acquire()
                mm = None
                for kc in range(8):
                    mm = k.op("tensor", lambda e, kc=kc, cc=cc, vh=vh, ps=ps:
                              e.matmul(ps[0:64, 0:512],
                                       xb.t(kc, cc * Lc, (cc + 1) * Lc),
                                       wv[:, kc * DM + vh * 512:kc * DM + (vh + 1) * 512],
                                       start=(kc == 0), stop=(kc == 7)),
                              deps=[txb, t_wv, ftok, st_tok] if kc == 0 else (),
                              inc=(kc == 7))
                tev = k.op("vector", lambda e, cc=cc, vh=vh, ps=ps: e.tensor_copy(
                    v_tm[0:64, cc * DM + vh * 512:cc * DM + (vh + 1) * 512],
                    ps[0:64, 0:512]), deps=[mm], inc=True)
                pool.release(i, tev)
                tv_toks.append(tev)
        if full:
            i, ps, ftok = pool.acquire()
            mm = None
            for kc in range(8):
                mm = k.op("tensor", lambda e, kc=kc, ps=ps: e.matmul(
                    ps[0:128, 0:Tc], wg1[:, kc * 128:(kc + 1) * 128],
                    xb.t(kc, 0, Tc), start=(kc == 0), stop=(kc == 7)),
                    deps=[txb, t_g1, ftok, t_th] if kc == 0 else (), inc=(kc == 7))
            t_gm = k.op("vector", lambda e, ps=ps, gmid=gmid: e.tensor_copy(
                gmid[:, 0:Tc], ps[0:128, 0:Tc]), deps=[mm], inc=True)
            pool.release(i, t_gm)
            t_gt = matmul_fm(k, pool, w=_FakeW(
                lambda kc, fo: wg2[:, fo * 128:(fo + 1) * 128], [t_g2]),
                nK=1, nF=8, T=Tc,
                rhs=lambda kc, a0, a1: gmid[:, a0:a1],
                evac=lambda e, ps, fo, a0, a1: e.activation(
                    out=gate_c.t(fo, a0, a1), in_=ps, func=AF.Sigmoid),
                evac_eng="scalar", mm_deps=[t_gm, t_th])
        for cc in range(NCH):
            for h in range(8):
                ecol = egend[:, h * NCH + cc:h * NCH + cc + 1]
                ecolK = egendK[:, h * NCH + cc:h * NCH + cc + 1]
                if full:
                    iA, psAm, fA = pool.acquire()
                    mmA = None
                    for hf in range(2):
                        mmA = k.op("tensor", lambda e, h=h, cc=cc, hf=hf, ps=psAm:
                                   e.matmul(ps[0:64, 0:Lc],
                                            kr.t(2 * h + hf, cc * Lc, (cc + 1) * Lc),
                                            qr.t(2 * h + hf, cc * Lc, (cc + 1) * Lc),
                                            start=(hf == 0), stop=(hf == 1)),
                                   deps=[t_dec, fA, t_cm] if hf == 0 else (),
                                   inc=(hf == 1))
                    tA = k.op("vector", lambda e, ps=psAm: e.tensor_tensor(
                        Ab[:, :], ps[0:64, 0:Lc], tri[:, :], op=OP.mult),
                        deps=[mmA], inc=True)
                    pool.release(iA, tA)
                    iO, psO, fO = pool.acquire()
                    mmO = k.op("tensor", lambda e, h=h, cc=cc, ps=psO:
                               e.matmul(ps[0:128, 0:Lc],
                                        v_tm[0:64, cc * DM + h * 128:cc * DM + (h + 1) * 128],
                                        Ab[:, :], start=True, stop=False),
                               deps=[tA, tv_toks[-1], fO], inc=False)
                    for hf in range(2):
                        mmO = k.op("tensor", lambda e, h=h, cc=cc, hf=hf, ps=psO: e.matmul(
                            ps[0:128, 0:Lc],
                            Sbf[:, (2 * h + hf) * 128:(2 * h + hf + 1) * 128],
                            qr.t(2 * h + hf, cc * Lc, (cc + 1) * Lc),
                            start=False, stop=(hf == 1)),
                            deps=[t_state], inc=(hf == 1))
                    tO = k.op("scalar", lambda e, h=h, cc=cc, ps=psO: e.activation(
                        out=o_c.t(h, cc * Lc, (cc + 1) * Lc), in_=ps[0:128, 0:Lc],
                        func=AF.Copy), deps=[mmO], inc=True)
                    pool.release(iO, tO)
                else:
                    mmO = None
                for hf in range(2):
                    t_ku = k.op("vector", lambda e, h=h, cc=cc, hf=hf, ecolK=ecolK:
                                e.tensor_scalar_mul(
                                    ku[:, 0:Lc], kr.t(2 * h + hf, cc * Lc, (cc + 1) * Lc),
                                    ecolK), deps=[t_dec, t_eg], inc=True)
                    mmT = k.op("tensor", lambda e: e.transpose(
                        pT[0:Lc, 0:128], ku[:, 0:Lc], cs.id[:, :]),
                        deps=[t_ku, pT_free[0]], inc=True)
                    tTe = k.op("vector", lambda e: e.tensor_copy(
                        kuT[0:Lc, 0:128], pT[0:Lc, 0:128]), deps=[mmT], inc=True)
                    pT_free[0] = tTe
                    iS, psS, fS = pool.acquire()
                    mmS = k.op("tensor", lambda e, h=h, cc=cc, ps=psS: e.matmul(
                        ps[0:128, 0:128], kuT[0:Lc, 0:128],
                        v_tm[0:64, cc * DM + h * 128:cc * DM + (h + 1) * 128],
                        start=True, stop=True), deps=[tTe, tv_toks[-1], fS], inc=True)
                    sl = slice((2 * h + hf) * 128, (2 * h + hf + 1) * 128)
                    k.op("vector", lambda e, sl=sl, ecol=ecol: e.tensor_scalar_mul(
                        S[:, sl], S[:, sl], ecol),
                        deps=[mmO if full else mmS, t_eg])
                    t_up = k.op("vector", lambda e, sl=sl, ps=psS: e.tensor_tensor(
                        S[:, sl], S[:, sl], ps[0:128, 0:128], op=OP.add),
                        deps=[mmS], inc=True)
                    pool.release(iS, t_up)
                    t_state = k.op("vector", lambda e, sl=sl: e.tensor_copy(
                        Sbf[:, sl], S[:, sl]), inc=True)
        t_tail = t_state
        if full:
            go_toks = []
            for j in range(8):
                t1 = k.op("vector", lambda e, j=j: e.tensor_tensor(
                    o_c.t(j), o_c.t(j), gate_c.t(j), op=OP.mult),
                    deps=[t_gt, k.fence("scalar")] if j == 0 else (), inc=True)
                go_toks.append(t1)
            of = gB
            of_toks = [k.op("vector", lambda e, j=j: e.tensor_copy(
                of.t(j), o_c.t(j)), deps=[go_toks[j]], inc=True) for j in range(8)]
            tok_n, _, guard = ln_chunk(k, pool, cs, r_fm=of, sq_bf=o_c, rb_bf=o_c,
                                       y_fm=of, yb_fm=o_c, stats_sb=stats,
                                       bias_sb=bias2, r_toks=of_toks, n=Tc,
                                       guard=guard,
                                       gcol=lambda j: nw[:, j:j + 1],
                                       bcol=lambda j: zcol[:])
            t_wo = matmul_fm(k, pool, w=wo, nK=8, nF=8, T=Tc,
                             rhs=lambda kc, a0, a1: o_c.t(kc, a0, a1),
                             evac=lambda e, ps, fo, a0, a1: e.activation(
                                 out=s_c.t(fo, a0, a1), in_=ps, func=AF.Copy),
                             evac_eng="scalar",
                             mm_deps=[tok_n] + ([st_tok] if st_tok else []))
            r_toks = [k.op("vector", lambda e, j=j: e.tensor_add(
                s_c.t(j), xcl.t(j), s_c.t(j)),
                deps=[t_wo] if j == 0 else (), inc=True) for j in range(8)]
            tok, _, guard = ln_chunk(k, pool, cs, r_fm=s_c, sq_bf=rb, rb_bf=rb,
                                     y_fm=s_c, yb_fm=None, stats_sb=stats,
                                     bias_sb=bias2, r_toks=r_toks, n=Tc,
                                     guard=guard)
            st_tok = store_fm(k, s_c, y_h, dst=(c0, c0 + Tc), deps=[tok],
                              eng="sync")
    if not full:
        k.dma("sync", S_h, S[:], deps=[t_tail])
        k.dma("sync", gs_h, gsum[:], deps=[k.fence("vector")])
    k.op("sync", lambda e: e.nop(), deps=[k.fence("sync", dma=True)])


def build_l1a_a(k):
    _build_l1a(k, full=False)


def build_l1a_b(k):
    _build_l1a(k, full=True)


def l1a_inputs(xslice, d, full, pred_ssum, pred_par, pred_S=None, pred_gsum=None):
    cmask = np.ones((128, 512), np.float32)
    cmask[:, 0::64] = 0.0
    tri = np.triu(np.ones((64, 64), np.float32)) * np.exp(55.0).astype(np.float32)
    m = {
        "x": np.ascontiguousarray(xslice),
        "wk": np.ascontiguousarray(d["L1_attn_wk"]),
        "wv": np.ascontiguousarray(d["L1_attn_wv"]),
        "cmask": cmask,
        "trimask": np.ascontiguousarray(tri).astype(bf16()),
        "pred_ssum": np.ascontiguousarray(pred_ssum),
        "pred_par": np.ascontiguousarray(pred_par),
        **const_inputs(),
    }
    if full:
        m.update({
            "wq": np.ascontiguousarray(d["L1_attn_wq"]),
            "wo": np.ascontiguousarray(d["L1_attn_wo"]),
            "wg1": np.ascontiguousarray(d["L1_attn_wg1"]),
            "wg2": np.ascontiguousarray(d["L1_attn_wg2"]),
            "norm_w": colify(d["L1_attn_norm_w"]),
            "pred_S": np.ascontiguousarray(pred_S),
            "pred_gsum": np.ascontiguousarray(pred_gsum),
        })
    return m


def build_l2a_a(k):
    T = TCORE
    x_h = k.dram_in("x", [DM, T], F32)
    p_h = k.dram_out("pooled", [DM, 8], F32)
    xcl = FM(k, "xcl", DM, 512, F32)
    pl = FM(k, "pl", DM, 8, F32)
    tok = None
    for ci in range(4):
        t = load_fm(k, xcl, x_h, src=(ci * 512, (ci + 1) * 512),
                    deps=[tok] if tok else ())
        for j in range(8):
            v3 = xcl.t(j).rearrange("p (c t) -> p c t", c=2)
            tok = k.op("vector", lambda e, j=j, ci=ci, v3=v3: e.tensor_reduce(
                pl.t(j, ci * 2, ci * 2 + 2), v3, axis=AX.X, op=OP.add),
                deps=[t] if j == 0 else (), inc=(j == 7))
    for j in range(8):
        tok = k.op("vector", lambda e, j=j: e.tensor_scalar_mul(
            pl.t(j), pl.t(j), 1.0 / 256.0), inc=(j == 7))
    store_fm(k, pl, p_h, deps=[tok])
    k.op("sync", lambda e: e.nop(), deps=[k.fence("sync", dma=True)])


def l2a_a_inputs(xslice):
    return {"x": np.ascontiguousarray(xslice), **const_inputs()}


L2NB = [8]
L2SECT = [99]


def build_l2a_b(k):
    T, CH, SK = TCORE, 256, 384
    NB = L2NB[0]
    cs = Consts(k)
    pool = PsumPool(k, 4, "pp")
    stage = StagePool(k, "stg", 8, F32)
    x_h = k.dram_in("x", [DM, T + 128], F32)
    y_h = k.dram_out("y", [DM, T], F32)
    pall_h = k.dram_in("pooled_all", [DM, 64], F32)
    wq = StreamW(k, "wqs", k.dram_in("qkv_q", [DM, DM], F32), DM, DM, stage)
    wk = StreamW(k, "wks", k.dram_in("qkv_k", [DM, DM], F32), DM, DM, stage)
    wv_h = k.dram_in("qkv_v", [DM, DM], F32)
    wout = StreamW(k, "wout", k.dram_in("out_w", [DM, DM], F32), DM, DM, stage)
    rq_h = k.dram_in("rq_w", [DM, 256], F32)
    rkv_h = k.dram_in("rkv_w", [DM, 512], F32)
    rout_h = k.dram_in("rout_w", [256, DM], F32)
    cols_h = k.dram_in("l2cols", [128, 40], F32)
    vb_h = k.dram_in("vb_row", [1, DM], BF16)
    glgr_h = k.dram_in("glgr", [128, 2 * 8 * 16], BF16)
    sel2_h = k.dram_in("sel2", [16, 8 * 128], BF16)
    mA_h = k.dram_in("maskA", [128, 2 * SK], F32)
    mA0_h = k.dram_in("maskA0", [128, 2 * SK], F32)
    latm_h = k.dram_in("latmask", [128, 8 * 64], F32)
    flagL_h = k.dram_in("flagL", [128, 2], F32)
    flagT_h = k.dram_in("flagT", [128, 8], F32)

    colsb = k.sbuf("l2colsb", [128, 40], F32)
    tcols = k.dma("sync", colsb[:], cols_h)
    O_QB, O_KB, O_RQB, O_RKVB, O_ROUTB, O_OUTB = 0, 8, 16, 18, 22, 30

    def col(o, i=0):
        return colsb[:, o + i:o + i + 1]

    wv = k.sbuf("wvsb", [128, 8 * DM], BF16)
    rqw = k.sbuf("rqw", [128, 8 * 256], BF16)
    rkvw = k.sbuf("rkvw", [128, 8 * 512], BF16)
    routw = k.sbuf("routw", [128, 2 * DM], BF16)
    glgr = k.sbuf("glgrb", [128, 2 * 8 * 16], BF16)
    sel2 = k.sbuf("sel2b", [16, 8 * 128], BF16)
    vbr = k.sbuf("vbr", [1, DM], BF16)
    mA = k.sbuf("mAb", [128, 2 * SK], F32)
    mA0 = k.sbuf("mA0b", [128, 2 * SK], F32)
    latm = k.sbuf("latmb", [128, 8 * 64], F32)
    flagL = k.sbuf("flagLb", [128, 2], F32)
    flagT = k.sbuf("flagTb", [128, 8], F32)
    for (sb, hb) in [(glgr, glgr_h), (sel2, sel2_h), (vbr, vb_h), (mA, mA_h),
                     (mA0, mA0_h), (latm, latm_h), (flagL, flagL_h),
                     (flagT, flagT_h)]:
        k.dma("sync", sb[:], hb)
    t_cm = k.fence("sync", dma=True)

    def stage_load(dst_sb, hbm, P, F, deps=()):
        si, raw, sfree = stage.acquire()
        dst = raw[0:P, 0:F]
        if len(hbm.ap) > 2:
            dst = dst.rearrange("p (a b) -> p a b", a=hbm.ap[1][1])
        t = k.dma("sync", dst, hbm, deps=([sfree] if sfree else []) + list(deps))
        tc = k.op("gpsimd", lambda e: e.tensor_copy(dst_sb[0:P, 0:F], raw[0:P, 0:F]),
                  deps=[t], inc=True)
        stage.release(si, tc)
        return tc
    for kc in range(8):
        stage_load(_FMSlice(wv, kc * DM, 128), wv_h[kc * 128:(kc + 1) * 128, :],
                   128, DM)
    for kc in range(8):
        stage_load(_FMSlice(rqw, kc * 256, 128), rq_h[kc * 128:(kc + 1) * 128, :],
                   128, 256)
        stage_load(_FMSlice(rkvw, kc * 512, 128), rkv_h[kc * 128:(kc + 1) * 128, :],
                   128, 512)
    for kc in range(2):
        t_small = stage_load(_FMSlice(routw, kc * DM, 128),
                             rout_h[kc * 128:(kc + 1) * 128, :], 128, DM)

    t_qb = k.op("vector", lambda e: e.tensor_scalar_mul(
        colsb[:, O_QB:O_QB + 8], colsb[:, O_QB:O_QB + 8], 0.125), deps=[tcols],
        inc=True)
    t_qb = k.op("vector", lambda e: e.tensor_scalar_mul(
        colsb[:, O_RQB:O_RQB + 2], colsb[:, O_RQB:O_RQB + 2], 0.125), inc=True)

    pall = FM(k, "pall", DM, 64, F32)
    pallb = FM(k, "pallb", DM, 64, BF16)
    t_pa = load_fm(k, pall, pall_h)
    t_pab = cast_fm(k, pallb, pall, deps=[t_pa])
    rkv = k.sbuf("rkv", [128, 4 * 64], BF16)
    t_rkv = None
    for fo in range(4):
        i, ps, ftok = pool.acquire()
        mm = None
        for kc in range(8):
            mm = k.op("tensor", lambda e, kc=kc, fo=fo, ps=ps: e.matmul(
                ps[0:128, 0:64], rkvw[:, kc * 512 + fo * 128:kc * 512 + (fo + 1) * 128],
                pallb.t(kc), start=(kc == 0), stop=(kc == 7)),
                deps=[t_pab, t_small, ftok, t_qb] if kc == 0 else (), inc=(kc == 7))
        t_rkv = k.op("vector", lambda e, fo=fo, ps=ps: e.tensor_scalar_add(
            rkv[:, fo * 64:(fo + 1) * 64], ps[0:128, 0:64], col(O_RKVB, fo)),
            deps=[mm], inc=True)
        pool.release(i, t_rkv)
    rvT = k.sbuf("rvT", [64, 4 * 64], BF16)
    pT = k.psum("pT", [128, 128], BF16)
    pT_free = [None]
    for lh in range(4):
        ro_ = (lh * 64) % 128
        src_ = rkv[ro_:ro_ + 64, (2 + lh // 2) * 64:(2 + lh // 2 + 1) * 64]
        mmT = k.op("tensor", lambda e, src_=src_, ro_=ro_: e.transpose(
            pT[0:64, 0:64], src_, cs.id[ro_:ro_ + 64, ro_:ro_ + 64]),
            deps=[t_rkv, pT_free[0], cs.tok], inc=True)
        t_rvT = k.op("vector", lambda e, lh=lh: e.tensor_copy(
            rvT[0:64, lh * 64:(lh + 1) * 64], pT[0:64, 0:64]), deps=[mmT],
            inc=True)
        pT_free[0] = t_rvT

    xcl = FM(k, "xcl", DM, SK, F32)
    xbe = FM(k, "xbe", DM, SK, BF16)
    qb = FM(k, "qb", DM, CH, BF16)
    kb = FM(k, "kb", DM, SK, BF16)
    v_tm = k.sbuf("vtm", [128, 3 * DM], BF16)
    rqb = FM(k, "rqb", 256, CH, BF16)
    sc1 = k.sbuf("sc1", [128, SK], F32)
    attn = k.sbuf("attnb", [128, SK], BF16)
    attnT = k.sbuf("attnT", [128, 3 * CH], BF16)
    o_loc = FM(k, "oloc", DM, CH, BF16)
    rem = FM(k, "rem", DM, CH, BF16)
    mixed = FM(k, "mixed", DM, CH, BF16)
    lat_o = FM(k, "lato", 256, CH, BF16)
    gem = k.sbuf("gem", [16, CH], BF16)
    gbc = k.sbuf("gbc", [128, CH], F32)
    s_c = FM(k, "scx", DM, CH, F32)
    rb = FM(k, "rbx", DM, CH, BF16)
    rows = k.sbuf("rows", [1, CH], F32)
    stats = k.sbuf("st", [1, 3 * CH], F32)
    bias2 = k.sbuf("bi", [128, 2 * CH], F32)
    tmpc = k.sbuf("tmpc", [128, 8], F32)

    guard = ()
    st_tok = None
    for b in range(NB):
        c0 = b * CH
        t_xl = load_fm(k, xcl, x_h, src=(c0, c0 + SK),
                       deps=[k.fence("tensor"), st_tok])
        txb = cast_fm(k, xbe, xcl, deps=[t_xl] + list(guard))
        mAb = mA0 if b == 0 else mA
        t_q = matmul_fm(k, pool, w=wq, nK=8, nF=8, T=CH,
                        rhs=lambda kc, a0, a1: xbe.t(kc, 128 + a0, 128 + a1),
                        evac=lambda e, ps, fo, a0, a1: e.tensor_scalar(
                            qb.t(fo, a0, a1), ps, 0.125, col(O_QB, fo),
                            op0=OP.mult, op1=OP.add),
                        evac_eng="vector", mm_deps=[txb, t_qb])
        t_k = matmul_fm(k, pool, w=wk, nK=8, nF=8, T=SK,
                        rhs=lambda kc, a0, a1: xbe.t(kc, a0, a1),
                        evac=lambda e, ps, fo, a0, a1: e.tensor_scalar_add(
                            kb.t(fo, a0, a1), ps, col(O_KB, fo)),
                        evac_eng="vector", mm_deps=[txb, tcols])
        tv = None
        for ts_i in range(3):
            for vh in range(2):
                i, ps, ftok = pool.acquire()
                for kc in range(8):
                    k.op("tensor", lambda e, kc=kc, ts_i=ts_i, vh=vh, ps=ps:
                         e.matmul(ps[0:128, 0:512],
                                  xbe.t(kc, ts_i * 128, (ts_i + 1) * 128),
                                  wv[:, kc * DM + vh * 512:kc * DM + (vh + 1) * 512],
                                  start=(kc == 0), stop=False),
                         deps=[txb, t_small, ftok, st_tok] if kc == 0 else (),
                         inc=False)
                mm = k.op("tensor", lambda e, vh=vh, ps=ps: e.matmul(
                    ps[0:128, 0:512], cs.orb[0:1, :],
                    vbr[0:1, vh * 512:(vh + 1) * 512], start=False, stop=True),
                    deps=[t_cm], inc=True)
                tv = k.op("vector", lambda e, ts_i=ts_i, vh=vh, ps=ps: e.tensor_copy(
                    v_tm[:, ts_i * DM + vh * 512:ts_i * DM + (vh + 1) * 512],
                    ps[0:128, 0:512]), deps=[mm], inc=True)
                pool.release(i, tv)
        if L2SECT[0] < 2:
            continue
        t_rq = matmul_fm(k, pool, w=_FakeW(
            lambda kc, fo: rqw[:, kc * 256 + fo * 128:kc * 256 + (fo + 1) * 128],
            [t_small]), nK=8, nF=2, T=CH,
            rhs=lambda kc, a0, a1: xbe.t(kc, 128 + a0, 128 + a1),
            evac=lambda e, ps, fo, a0, a1: e.tensor_scalar(
                rqb.t(fo, a0, a1), ps, 0.125, col(O_RQB, fo),
                op0=OP.mult, op1=OP.add),
            evac_eng="vector", mm_deps=[txb, t_qb])
        if L2SECT[0] < 3:
            continue
        t_oloc = None
        for h in range(16):
            jt, ro = h // 2, (h % 2) * 64
            io, pso, foo = pool.acquire()
            mmo_started = False
            for tsub in range(2):
                i1, ps1, f1 = pool.acquire()
                mm = k.op("tensor", lambda e, tsub=tsub, ps=ps1, jt=jt, ro=ro:
                          e.matmul(ps[0:128, 0:SK],
                                   qb.t(jt, tsub * 128, (tsub + 1) * 128,
                                        r0=ro, r1=ro + 64),
                                   kb.t(jt, 0, SK, r0=ro, r1=ro + 64),
                                   start=True, stop=True),
                          deps=[t_q, t_k, f1, t_oloc], inc=True)
                t_m = k.op("vector", lambda e, tsub=tsub, ps=ps1, mAb=mAb:
                           e.tensor_tensor(sc1[:, 0:SK], ps[0:128, 0:SK],
                                           mAb[:, tsub * SK:(tsub + 1) * SK],
                                           op=OP.add), deps=[mm, t_cm], inc=True)
                pool.release(i1, t_m)
                k.op("vector", lambda e: e.reduce_max(tmpc[:, 0:1], sc1[:, 0:SK],
                                                      axis=AX.X))
                t_n = k.op("vector", lambda e: e.tensor_scalar_mul(
                    tmpc[:, 1:2], tmpc[:, 0:1], -1.0), inc=True)
                t_z = k.op("scalar", lambda e: e.activation(
                    out=attn[:, 0:SK], in_=sc1[:, 0:SK], func=AF.Exp,
                    bias=tmpc[:, 1:2], accum_out=tmpc[:, 2:3]),
                    deps=[t_n], inc=True)
                k.op("vector", lambda e: e.reciprocal(tmpc[:, 3:4], tmpc[:, 2:3]),
                     deps=[t_z])
                t_at = k.op("vector", lambda e: e.tensor_scalar_mul(
                    attn[:, 0:SK], attn[:, 0:SK], tmpc[:, 3:4]), inc=True)
                if b == 0:
                    t_at = k.op("vector", lambda e, tsub=tsub: e.tensor_scalar_mul(
                        attn[:, 0:SK], attn[:, 0:SK], flagL[:, tsub:tsub + 1]),
                        deps=[t_cm], inc=True)
                for ss in range(3):
                    mmT = k.op("tensor", lambda e, ss=ss: e.transpose(
                        pT[0:128, 0:128], attn[:, ss * 128:(ss + 1) * 128],
                        cs.id[:, :]), deps=[t_at, pT_free[0]], inc=True)
                    tTT = k.op("vector", lambda e, ss=ss, tsub=tsub: e.tensor_copy(
                        attnT[:, ss * CH + tsub * 128:ss * CH + (tsub + 1) * 128],
                        pT[0:128, 0:128]), deps=[mmT], inc=True)
                    pT_free[0] = tTT
            mmo = None
            for ss in range(3):
                mmo = k.op("tensor", lambda e, ss=ss, h=h, ps=pso: e.matmul(
                    ps[0:64, 0:CH], v_tm[:, ss * DM + h * 64:ss * DM + (h + 1) * 64],
                    attnT[:, ss * CH:(ss + 1) * CH], start=(ss == 0), stop=(ss == 2)),
                    deps=[pT_free[0], tv, foo] if ss == 0 else (), inc=(ss == 2))
            t_oloc = k.op("scalar", lambda e, h=h, jt=jt, ro=ro, ps=pso: e.activation(
                out=o_loc.t(jt, 0, CH, r0=ro, r1=ro + 64), in_=ps[0:64, 0:CH],
                func=AF.Copy), deps=[mmo], inc=True)
            pool.release(io, t_oloc)
        if L2SECT[0] < 4:
            continue
        t_lat = None
        for lh in range(4):
            jt, ro = (lh * 64) // 128, (lh * 64) % 128
            il, psl, fl = pool.acquire()
            for tsub in range(2):
                mm = k.op("tensor", lambda e, lh=lh, tsub=tsub, ps=psl, jt=jt, ro=ro:
                          e.matmul(ps[0:128, 0:64],
                                   rqb.t(jt, tsub * 128, (tsub + 1) * 128,
                                         r0=ro, r1=ro + 64),
                                   rkv[ro:ro + 64, (lh // 2) * 64:(lh // 2 + 1) * 64],
                                   start=True, stop=True),
                          deps=[t_rq, t_rkv, fl, t_lat], inc=True)
                t_m = k.op("vector", lambda e, b=b, ps=psl: e.tensor_tensor(
                    sc1[:, 0:64], ps[0:128, 0:64], latm[:, b * 64:(b + 1) * 64],
                    op=OP.add), deps=[mm, t_cm], inc=True)
                k.op("vector", lambda e: e.reduce_max(tmpc[:, 0:1], sc1[:, 0:64],
                                                      axis=AX.X))
                t_n = k.op("vector", lambda e: e.tensor_scalar_mul(
                    tmpc[:, 1:2], tmpc[:, 0:1], -1.0), inc=True)
                t_z = k.op("scalar", lambda e: e.activation(
                    out=attn[:, 0:64], in_=sc1[:, 0:64], func=AF.Exp,
                    bias=tmpc[:, 1:2], accum_out=tmpc[:, 2:3]),
                    deps=[t_n], inc=True)
                k.op("vector", lambda e: e.reciprocal(tmpc[:, 3:4], tmpc[:, 2:3]),
                     deps=[t_z])
                k.op("vector", lambda e: e.tensor_scalar_mul(
                    attn[:, 0:64], attn[:, 0:64], tmpc[:, 3:4]))
                t_at = k.op("vector", lambda e, b=b: e.tensor_scalar_mul(
                    attn[:, 0:64], attn[:, 0:64], flagT[:, b:b + 1]),
                    deps=[t_cm], inc=True)
                mmT = k.op("tensor", lambda e: e.transpose(
                    pT[0:64, 0:128], attn[:, 0:64], cs.id[:, :]),
                    deps=[t_at, pT_free[0]], inc=True)
                tTT = k.op("vector", lambda e, tsub=tsub: e.tensor_copy(
                    attnT[0:64, tsub * 128:(tsub + 1) * 128], pT[0:64, 0:128]),
                    deps=[mmT], inc=True)
                pT_free[0] = tTT
            mml = k.op("tensor", lambda e, lh=lh, ps=psl: e.matmul(
                ps[0:64, 0:CH], rvT[0:64, lh * 64:(lh + 1) * 64],
                attnT[0:64, 0:CH], start=True, stop=True),
                deps=[pT_free[0]], inc=True)
            t_lat = k.op("scalar", lambda e, lh=lh, jt=jt, ro=ro, ps=psl: e.activation(
                out=lat_o.t(jt, 0, CH, r0=ro, r1=ro + 64), in_=ps[0:64, 0:CH],
                func=AF.Copy), deps=[mml], inc=True)
            pool.release(il, t_lat)
        if L2SECT[0] < 5:
            continue
        t_rem = matmul_fm(k, pool, w=_FakeW(
            lambda kc, fo: routw[:, kc * DM + fo * 128:kc * DM + (fo + 1) * 128],
            [t_small]), nK=2, nF=8, T=CH,
            rhs=lambda kc, a0, a1: lat_o.t(kc, a0, a1),
            evac=lambda e, ps, fo, a0, a1: e.tensor_scalar_add(
                rem.t(fo, a0, a1), ps, col(O_ROUTB, fo)),
            evac_eng="vector", mm_deps=[t_lat])
        if L2SECT[0] < 6:
            continue
        ig, psg, fg = pool.acquire()
        mmg = None
        for j in range(8):
            mmg = k.op("tensor", lambda e, j=j, ps=psg: e.matmul(
                ps[0:16, 0:CH], glgr[:, j * 16:(j + 1) * 16],
                o_loc.t(j), start=(j == 0), stop=False),
                deps=[t_oloc, t_cm, fg, t_rem] if j == 0 else (), inc=False)
        for j in range(8):
            mmg = k.op("tensor", lambda e, j=j, ps=psg: e.matmul(
                ps[0:16, 0:CH], glgr[:, 128 + j * 16:128 + (j + 1) * 16],
                rem.t(j), start=False, stop=(j == 7)),
                deps=[t_rem] if j == 0 else (), inc=(j == 7))
        t_g = k.op("scalar", lambda e, ps=psg: e.activation(
            out=gem[0:16, 0:CH], in_=ps[0:16, 0:CH], func=AF.Sigmoid),
            deps=[mmg], inc=True)
        pool.release(ig, t_g)
        t_mx = None
        for j in range(8):
            ib, psb, fb = pool.acquire()
            mmb = k.op("tensor", lambda e, j=j, ps=psb: e.matmul(
                ps[0:128, 0:CH], sel2[0:16, j * 128:(j + 1) * 128], gem[0:16, 0:CH],
                start=True, stop=True), deps=[t_g, fb, t_mx], inc=True)
            tgb = k.op("vector", lambda e, ps=psb: e.tensor_copy(
                gbc[:, 0:CH], ps[0:128, 0:CH]), deps=[mmb], inc=True)
            pool.release(ib, tgb)
            k.op("vector", lambda e, j=j: e.tensor_tensor(
                mixed.t(j), o_loc.t(j), rem.t(j), op=OP.subtract))
            k.op("vector", lambda e, j=j: e.tensor_tensor(
                mixed.t(j), mixed.t(j), gbc[:, 0:CH], op=OP.mult))
            t_mx = k.op("vector", lambda e, j=j: e.tensor_tensor(
                mixed.t(j), mixed.t(j), rem.t(j), op=OP.add), inc=True)
        t_o = matmul_fm(k, pool, w=wout, nK=8, nF=8, T=CH,
                        rhs=lambda kc, a0, a1: mixed.t(kc, a0, a1),
                        evac=lambda e, ps, fo, a0, a1: e.tensor_scalar_add(
                            s_c.t(fo, a0, a1), ps, col(O_OUTB, fo)),
                        evac_eng="vector",
                        mm_deps=[t_mx] + ([st_tok] if st_tok else []))
        r_toks = [k.op("vector", lambda e, j=j: e.tensor_add(
            s_c.t(j), xcl.t(j, 128, 128 + CH), s_c.t(j)),
            deps=[t_o] if j == 0 else (), inc=True) for j in range(8)]
        tok, _, guard = ln_chunk(k, pool, cs, r_fm=s_c, sq_bf=rb, rb_bf=rb,
                                 y_fm=s_c, yb_fm=None, stats_sb=stats,
                                 bias_sb=bias2, r_toks=r_toks, n=CH, guard=guard)
        st_tok = store_fm(k, s_c, y_h, dst=(c0, c0 + CH), deps=[tok], eng="sync")
    k.op("sync", lambda e: e.nop(), deps=[k.fence("sync", dma=True)])


def l2a_b_inputs(xslice_halo, d, pooled_all, core_id):
    qkv = d["L2_attn_qkv_w"]
    qkvb = d["L2_attn_qkv_b"]
    cols = np.zeros((128, 40), np.float32)
    cols[:, 0:8] = colify(qkvb[0:1024])
    cols[:, 8:16] = colify(qkvb[1024:2048])
    cols[:, 16:18] = colify(d["L2_attn_rq_b"])
    cols[:, 18:22] = colify(d["L2_attn_rkv_b"])
    cols[:, 22:30] = colify(d["L2_attn_rout_b"])
    cols[:, 30:38] = colify(d["L2_attn_out_b"])
    gl = d["L2_attn_gl_w"][:, 0]
    gr = d["L2_attn_gr_w"][:, 0]
    glgr = np.zeros((128, 2 * 8 * 16), np.float32)
    for j in range(8):
        glgr[0:64, j * 16 + 2 * j] = gl
        glgr[64:128, j * 16 + 2 * j + 1] = gl
        glgr[0:64, 128 + j * 16 + 2 * j] = gr
        glgr[64:128, 128 + j * 16 + 2 * j + 1] = gr
    sel2 = np.zeros((16, 8 * 128), np.float32)
    for j in range(8):
        for p in range(128):
            sel2[2 * j + (p >= 64), j * 128 + p] = 1.0
    NEG = -1e9
    t_idx = np.arange(256)
    s_idx = np.arange(SKC := 384)
    mA = np.zeros((2, 128, 384), np.float32)
    for tsub in range(2):
        tt = tsub * 128 + t_idx[0:128]
        dist = tt[:, None] - (s_idx[None, :] - 128)
        bad = (dist <= 0) | (dist > 128)
        mA[tsub] = np.where(bad, NEG, 0.0)
    mA0 = mA.copy()
    if core_id == 0:
        kpos = s_idx - 128
        mA0[:, :, :][:, :, kpos < 0] = NEG
    mAf = np.concatenate([mA[0], mA[1]], axis=1)
    mA0f = np.concatenate([mA0[0], mA0[1]], axis=1)
    latm = np.zeros((128, 8 * 64), np.float32)
    for b in range(8):
        gc = core_id * 8 + b
        latm[:, b * 64:(b + 1) * 64] = np.where(np.arange(64)[None, :] < gc, 0.0, NEG)
    flagL = np.ones((128, 2), np.float32)
    flagT = np.ones((128, 8), np.float32)
    if core_id == 0:
        flagL[0, 0] = 0.0
        flagT[:, 0] = 0.0
    b = bf16()
    return {
        "x": np.ascontiguousarray(xslice_halo),
        "pooled_all": np.ascontiguousarray(pooled_all),
        "qkv_q": np.ascontiguousarray(qkv[:, 0:1024]),
        "qkv_k": np.ascontiguousarray(qkv[:, 1024:2048]),
        "qkv_v": np.ascontiguousarray(qkv[:, 2048:3072]),
        "out_w": np.ascontiguousarray(d["L2_attn_out_w"]),
        "rq_w": np.ascontiguousarray(d["L2_attn_rq_w"]),
        "rkv_w": np.ascontiguousarray(d["L2_attn_rkv_w"]),
        "rout_w": np.ascontiguousarray(d["L2_attn_rout_w"]),
        "l2cols": cols,
        "vb_row": np.ascontiguousarray(qkvb[2048:3072])[None, :].astype(b),
        "glgr": glgr.astype(b),
        "sel2": sel2.astype(b),
        "maskA": mAf,
        "maskA0": mA0f,
        "latmask": latm,
        "flagL": flagL,
        "flagT": flagT,
        **const_inputs(),
    }


class ShiftFM:
    pass


def build_l1f_attn(k):
    T, W = TCORE, 512
    NW = T // W
    cs = Consts(k)
    pool = PsumPool(k, 4, "pp")
    stage = StagePool(k, "stg", 8, F32)
    x_h = k.dram_in("x", [DM, T + 513], F32)
    y_h = k.dram_out("y", [DM, T], F32)
    wqp = StreamW(k, "wqp", k.dram_in("wq_part", [DM, 512], F32), DM, 512, stage)
    wkp = StreamW(k, "wkp", k.dram_in("wk_part", [DM, 512], F32), DM, 512, stage)
    wvp_h = k.dram_in("wv_part", [DM, 512], F32)
    wo = StreamW(k, "wos", k.dram_in("wo_l", [512, DM], F32), 512, DM, stage)
    trow_h = k.dram_in("t_row", [1, 2560], F32)
    ktab_h = k.dram_in("ktab", [128, 2560], F32)
    k2tab_h = k.dram_in("k2tab", [128, 2560], F32)
    mA_h = k.dram_in("maskA", [128, 4 * 1024], BF16)
    mA0_h = k.dram_in("maskA0", [128, 4 * 1024], BF16)
    cols_h = k.dram_in("l1fcols", [128, 16], F32)
    bo_h = k.dram_in("blockones", [128, 4 * 8], BF16)
    sel_h = k.dram_in("selh", [8, 4 * 128], BF16)

    colsb = k.sbuf("l1fcolsb", [128, 16], F32)
    bo = k.sbuf("bo", [128, 32], BF16)
    sel = k.sbuf("selb", [8, 512], BF16)
    mA = k.sbuf("mAb", [128, 4096], BF16)
    mA0 = k.sbuf("mA0b", [128, 4096], BF16)
    trow = k.sbuf("trowb", [1, 2560], F32)
    xcl = FM(k, "xcl", DM, 1025, F32)
    ktab = xcl.sb[:, 0:2560]
    k2tab = xcl.sb[:, 2560:5120]
    for sb, hb in [(colsb, cols_h), (bo, bo_h), (sel, sel_h), (mA, mA_h),
                   (mA0, mA0_h), (trow, trow_h), (ktab, ktab_h), (k2tab, k2tab_h)]:
        k.dma("sync", sb[:], hb)
    t_cm = k.fence("sync", dma=True)
    O_IF, O_LNB, O_QS, O_KS = 0, 1, 2, 6

    def col(o, i=0):
        return colsb[:, o + i:o + i + 1]

    wv = k.sbuf("wvsb", [128, 8 * 512], BF16)

    def stage_load(dst_sb, hbm, P, F, deps=()):
        si, raw, sfree = stage.acquire()
        t = k.dma("sync", raw[0:P, 0:F], hbm,
                  deps=([sfree] if sfree else []) + list(deps))
        tc = k.op("gpsimd", lambda e: e.tensor_copy(dst_sb[0:P, 0:F], raw[0:P, 0:F]),
                  deps=[t], inc=True)
        stage.release(si, tc)
        return tc
    t_wv = None
    for kc in range(8):
        t_wv = stage_load(_FMSlice(wv, kc * 512, 128),
                          wvp_h[kc * 128:(kc + 1) * 128, :], 128, 512)

    TB = 2560
    tabf = k.sbuf("tabf", [128, TB], F32)
    tabf2 = k.sbuf("tabf2", [128, TB], F32)
    tabf3 = k.sbuf("tabf3", [128, TB], F32)
    csq = k.sbuf("csq", [128, TB], BF16)
    ssq = k.sbuf("ssq", [128, TB], BF16)
    csk = k.sbuf("csk", [128, TB], BF16)
    ssk = k.sbuf("ssk", [128, TB], BF16)
    TPA = 6.28125
    TPB = float(np.float32(2 * np.pi - 6.28125))
    for c0 in range(0, TB, 512):
        t_tb = bcast_chunk(k, pool, cs, tabf2[:, c0:c0 + 512],
                           trow[0:1, c0:c0 + 512], 128, [t_cm], n=512)
    k.op("vector", lambda e: e.tensor_scalar(
        tabf3[:, :], tabf2[:, :], -8192.0, 1.0 / 256.0, op0=OP.add, op1=OP.mult),
        deps=[t_tb, t_cm])
    t_pw = k.op("vector", lambda e: e.tensor_scalar_mul(
        tabf3[:, :], tabf3[:, :], col(O_LNB)), inc=True)
    t_sc = k.op("scalar", lambda e: e.activation(out=tabf[:, :], in_=tabf3[:, :],
                                                 func=AF.Exp), deps=[t_pw], inc=True)
    k.op("vector", lambda e: e.tensor_scalar_mul(tabf2[:, :], tabf2[:, :],
                                                 col(O_IF)), deps=[t_sc])
    k.op("vector", lambda e: e.tensor_scalar_mul(tabf3[:, :], ktab[:, :], TPA))
    k.op("vector", lambda e: e.tensor_tensor(tabf3[:, :], tabf2[:, :], tabf3[:, :],
                                             op=OP.subtract))
    k.op("vector", lambda e: e.tensor_scalar_mul(sc1g[:, :] if False else csq[:, :],
                                                 ktab[:, :], TPB))
    t_red = k.op("vector", lambda e: e.scalar_tensor_tensor(
        tabf3[:, :], ktab[:, :], -TPB, tabf3[:, :], op0=OP.mult, op1=OP.add),
        inc=True)
    t_sin = k.op("scalar", lambda e: e.activation(out=tabf3[:, :], in_=tabf3[:, :],
                                                  func=AF.Sin), deps=[t_red],
                 inc=True)
    k.op("vector", lambda e: e.tensor_tensor(ssq[:, :], tabf3[:, :], tabf[:, :],
                                             op=OP.mult), deps=[t_sin, t_sc])
    k.op("vector", lambda e: e.tensor_scalar_mul(ssq[:, :], ssq[:, :], 8.0))
    k.op("vector", lambda e: e.reciprocal(tabf2[:, :] if False else tabf[:, :],
                                          tabf[:, :]))
    t_ssk = k.op("vector", lambda e: e.tensor_tensor(ssk[:, :], tabf3[:, :],
                                                     tabf[:, :], op=OP.mult),
                 inc=True)
    k.op("vector", lambda e: e.tensor_scalar_add(
        tabf2[:, :], tabf2[:, :], float(np.float32(np.pi / 2))), deps=[t_ssk])
    k.op("vector", lambda e: e.scalar_tensor_tensor(
        tabf2[:, :], k2tab[:, :], -TPA, tabf2[:, :], op0=OP.mult, op1=OP.add))
    t_red2 = k.op("vector", lambda e: e.scalar_tensor_tensor(
        tabf2[:, :], k2tab[:, :], -TPB, tabf2[:, :], op0=OP.mult, op1=OP.add),
        inc=True)
    t_cos = k.op("scalar", lambda e: e.activation(out=tabf2[:, :], in_=tabf2[:, :],
                                                  func=AF.Sin), deps=[t_red2],
                 inc=True)
    k.op("vector", lambda e: e.tensor_tensor(csk[:, :], tabf2[:, :], tabf[:, :],
                                             op=OP.mult), deps=[t_cos])
    k.op("vector", lambda e: e.reciprocal(tabf[:, :], tabf[:, :]))
    k.op("vector", lambda e: e.tensor_tensor(csq[:, :], tabf2[:, :], tabf[:, :],
                                             op=OP.mult))
    t_tab = k.op("vector", lambda e: e.tensor_scalar_mul(csq[:, :], csq[:, :], 8.0),
                 inc=True)

    xbe = FM(k, "xbe", DM, 1025, BF16)
    qf = FM(k, "qf", 512, 1024, F32)
    q2 = FM(k, "q2", 512, 512, BF16)
    k2b = FM(k, "k2b", 512, 1024, BF16)
    v_tm = k.sbuf("vtm", [128, 8 * 512], BF16)
    sc1 = tabf3[:, 1024:2048]
    attn = k.sbuf("attnb", [128, 1024], BF16)
    attnT = k.sbuf("attnT", [128, 8 * 512], BF16)
    o_c = FM(k, "oc", 512, W, BF16)
    s_c = ViewFM(qf.sb, DM, W)
    rs8 = k.sbuf("rs8", [8, 1024], F32)
    rsb = None
    scr = tabf3.sb[:, 0:1024] if hasattr(tabf3, "sb") else tabf3[:, 0:1024]
    scrb = attn
    tmpc = k.sbuf("tmpc", [128, 8], F32)
    pT = k.psum("pT", [128, 128], BF16)
    pT_free = [None]
    st_tok = None

    def proj_block(w, dst_f32, ncols, coff, deps0):
        toks = []
        t = None
        for fo in range(4):
            lhsT, ext = w.prep(k, fo)
            mm = None
            for cc0 in range(0, ncols, 512):
                i, ps, ftok = pool.acquire()
                mm = None
                for kc in range(8):
                    off = coff + (0 if kc < 4 else -1)
                    rap = xbe.t(kc, cc0 + off, cc0 + off + 512)
                    mm = k.op("tensor", lambda e, rap=rap, lhsT=lhsT, kc=kc, fo=fo,
                              ps=ps: e.matmul(ps[0:128, 0:512], lhsT(kc, fo), rap,
                                              start=(kc == 0), stop=(kc == 7)),
                              deps=(list(deps0) + ext + [ftok]) if kc == 0 else (),
                              inc=(kc == 7))
                t = k.op("scalar", lambda e, fo=fo, cc0=cc0, ps=ps, dst_f32=dst_f32:
                         e.activation(out=dst_f32.t(fo, cc0, cc0 + 512),
                                      in_=ps[0:128, 0:512], func=AF.Copy),
                         deps=[mm], inc=True)
                pool.release(i, t)
            w.done(k, fo, mm)
            toks.append(t)
        return toks

    def norm_rope(src_f32, dst_bf, ncols, tab_off, cstab, sstab, scale_o, deps0):
        t4 = None
        prev_mm = None
        for cc0 in range(0, ncols, 512):
            i1, ps1, f1 = pool.acquire()
            mm = None
            for j in range(4):
                tsq = k.op("scalar", lambda e, j=j, cc0=cc0: e.activation(
                    out=scrb[:, 0:512], in_=src_f32.t(j, cc0, cc0 + 512),
                    func=AF.Square),
                    deps=(list(deps0) + [k.fence("tensor")]
                          if (j == 0 and cc0 == 0) else []) +
                    ([prev_mm] if prev_mm else []), inc=True)
                mm = k.op("tensor", lambda e, j=j, ps=ps1: e.matmul(
                    ps[0:8, 0:512], bo[:, j * 8:(j + 1) * 8], scrb[:, 0:512],
                    start=(j == 0), stop=(j == 3)), deps=[tsq, f1, t_cm],
                    inc=True)
                prev_mm = mm
            t1 = k.op("scalar", lambda e, ps=ps1, cc0=cc0: e.activation(
                out=rs8[0:8, cc0:cc0 + 512], in_=ps[0:8, 0:512], func=AF.Sqrt),
                deps=[mm], inc=True)
            pool.release(i1, t1)
            k.op("vector", lambda e, cc0=cc0: e.tensor_scalar_max(
                rs8[0:8, cc0:cc0 + 512], rs8[0:8, cc0:cc0 + 512], 1e-12),
                deps=[t1])
            k.op("vector", lambda e, cc0=cc0: e.reciprocal(
                rs8[0:8, cc0:cc0 + 512], rs8[0:8, cc0:cc0 + 512]))
            t2 = k.op("vector", lambda e, cc0=cc0: e.tensor_copy(
                attnT[0:8, cc0:cc0 + 512], rs8[0:8, cc0:cc0 + 512]),
                deps=[k.fence("tensor")], inc=True)
            for j in range(4):
                i2, ps2, f2 = pool.acquire()
                mmb = k.op("tensor", lambda e, j=j, cc0=cc0, ps=ps2: e.matmul(
                    ps[0:128, 0:512], sel[0:8, j * 128:(j + 1) * 128],
                    attnT[0:8, cc0:cc0 + 512], start=True, stop=True),
                    deps=[t2, f2], inc=True)
                t3 = k.op("vector", lambda e, j=j, cc0=cc0, ps=ps2:
                          e.scalar_tensor_tensor(scr[:, 0:512],
                                                 src_f32.t(j, cc0, cc0 + 512),
                                                 col(scale_o, j), ps[0:128, 0:512],
                                                 op0=OP.mult, op1=OP.mult),
                          deps=[mmb], inc=True)
                pool.release(i2, t3)
                k.op("vector", lambda e: e.tensor_scalar_mul(
                    sc1[0:64, 0:512], scr[64:128, 0:512], -1.0))
                k.op("vector", lambda e: e.tensor_copy(
                    sc1[64:128, 0:512], scr[0:64, 0:512]))
                tco = tab_off + cc0
                k.op("vector", lambda e, tco=tco, cstab=cstab: e.tensor_tensor(
                    scr[:, 0:512], scr[:, 0:512], cstab[:, tco:tco + 512],
                    op=OP.mult), deps=[t_tab])
                k.op("vector", lambda e, tco=tco, sstab=sstab: e.tensor_tensor(
                    sc1[:, 0:512], sc1[:, 0:512], sstab[:, tco:tco + 512],
                    op=OP.mult))
                t4 = k.op("vector", lambda e, j=j, cc0=cc0, dst_bf=dst_bf:
                          e.tensor_tensor(dst_bf.t(j, cc0, cc0 + 512),
                                          scr[:, 0:512], sc1[:, 0:512], op=OP.add),
                          inc=True)
        return t4

    for w in range(NW):
        c0 = w * W
        t_xl = load_fm(k, xcl, x_h, src=(c0, c0 + 1025),
                       deps=[k.fence("tensor"), st_tok, t_tab])
        txb = None
        for j in range(8):
            txb = k.op("vector", lambda e, j=j: e.tensor_copy(
                xbe.t(j), xcl.t(j)), deps=[t_xl] if j == 0 else (), inc=(j == 7))
        tk = proj_block(wkp, qf, 1024, 1, [txb])
        t_k2 = norm_rope(qf, k2b, 1024, w * 512, csk, ssk, O_KS, [tk[-1]])
        tv = None
        for tt in range(8):
            i, ps, ftok = pool.acquire()
            mm = None
            for kc in range(8):
                off = 1 + (0 if kc < 4 else -1)
                rap = xbe.t(kc, tt * 128 + off, (tt + 1) * 128 + off)
                mm = k.op("tensor", lambda e, rap=rap, kc=kc, tt=tt, ps=ps:
                          e.matmul(ps[0:128, 0:512], rap,
                                   wv[:, kc * 512:(kc + 1) * 512],
                                   start=(kc == 0), stop=(kc == 7)),
                          deps=[txb, t_wv, ftok, st_tok] if kc == 0 else (),
                          inc=(kc == 7))
            tv = k.op("vector", lambda e, tt=tt, ps=ps: e.tensor_copy(
                v_tm[:, tt * 512:(tt + 1) * 512], ps[0:128, 0:512]),
                deps=[mm], inc=True)
            pool.release(i, tv)
        tq = proj_block(wqp, qf, 512, 513, [t_k2])
        t_q2 = norm_rope(qf, q2, 512, w * 512 + 512, csq, ssq, O_QS, [tq[-1]])
        mAb = mA0 if w == 0 else mA
        t_o = None
        for h in range(8):
            jt, ro = h // 2, (h % 2) * 64
            for tsub in range(4):
                for half in range(2):
                    i1, ps1, f1 = pool.acquire()
                    mm = k.op("tensor", lambda e, tsub=tsub, half=half, ps=ps1,
                              jt=jt, ro=ro: e.matmul(
                                  ps[0:128, 0:512],
                                  q2.t(jt, tsub * 128, (tsub + 1) * 128,
                                       r0=ro, r1=ro + 64),
                                  k2b.t(jt, half * 512, (half + 1) * 512,
                                        r0=ro, r1=ro + 64),
                                  start=True, stop=True),
                              deps=[t_q2, t_k2, f1, t_o], inc=True)
                    t_m = k.op("vector", lambda e, tsub=tsub, half=half, ps=ps1,
                               mAb=mAb: e.tensor_tensor(
                                   sc1[:, half * 512:(half + 1) * 512],
                                   ps[0:128, 0:512],
                                   mAb[:, tsub * 1024 + half * 512:
                                       tsub * 1024 + (half + 1) * 512],
                                   op=OP.add), deps=[mm, t_cm], inc=True)
                    pool.release(i1, t_m)
                k.op("vector", lambda e: e.reduce_max(tmpc[:, 0:1], sc1[:, 0:1024],
                                                      axis=AX.X))
                t_n = k.op("vector", lambda e: e.tensor_scalar_mul(
                    tmpc[:, 1:2], tmpc[:, 0:1], -1.0), inc=True)
                t_z = k.op("scalar", lambda e: e.activation(
                    out=attn[:, 0:1024], in_=sc1[:, 0:1024], func=AF.Exp,
                    bias=tmpc[:, 1:2], accum_out=tmpc[:, 2:3]),
                    deps=[t_n], inc=True)
                k.op("vector", lambda e: e.reciprocal(tmpc[:, 3:4], tmpc[:, 2:3]),
                     deps=[t_z])
                t_at = k.op("vector", lambda e: e.tensor_scalar_mul(
                    attn[:, 0:1024], attn[:, 0:1024], tmpc[:, 3:4]), inc=True)
                for ss in range(8):
                    mmT = k.op("tensor", lambda e, ss=ss: e.transpose(
                        pT[0:128, 0:128], attn[:, ss * 128:(ss + 1) * 128],
                        cs.id[:, :]), deps=[t_at, pT_free[0]], inc=True)
                    tTT = k.op("vector", lambda e, ss=ss, tsub=tsub: e.tensor_copy(
                        attnT[:, ss * 512 + tsub * 128:ss * 512 + (tsub + 1) * 128],
                        pT[0:128, 0:128]), deps=[mmT], inc=True)
                    pT_free[0] = tTT
            io, pso, foo = pool.acquire()
            mmo = None
            for ss in range(8):
                mmo = k.op("tensor", lambda e, ss=ss, h=h, ps=pso: e.matmul(
                    ps[0:64, 0:512],
                    v_tm[:, ss * 512 + h * 64:ss * 512 + (h + 1) * 64],
                    attnT[:, ss * 512:(ss + 1) * 512], start=(ss == 0),
                    stop=(ss == 7)),
                    deps=[pT_free[0], tv, foo] if ss == 0 else (), inc=(ss == 7))
            t_o = k.op("scalar", lambda e, h=h, jt=jt, ro=ro, ps=pso: e.activation(
                out=o_c.t(jt, 0, W, r0=ro, r1=ro + 64), in_=ps[0:64, 0:W],
                func=AF.Copy), deps=[mmo], inc=True)
            pool.release(io, t_o)
        t_wo = matmul_fm(k, pool, w=wo, nK=4, nF=8, T=W,
                         rhs=lambda kc, a0, a1: o_c.t(kc, a0, a1),
                         evac=lambda e, ps, fo, a0, a1: e.activation(
                             out=s_c.t(fo, a0, a1), in_=ps, func=AF.Copy),
                         evac_eng="scalar",
                         mm_deps=[k.fence("scalar")] + ([st_tok] if st_tok else []))
        r_last = None
        for j in range(8):
            r_last = k.op("vector", lambda e, j=j: e.tensor_add(
                s_c.t(j), xcl.t(j, 513, 513 + W), s_c.t(j)),
                deps=[t_wo] if j == 0 else (), inc=(j == 7))
        st_tok = store_fm(k, s_c, y_h, dst=(c0, c0 + W), deps=[r_last], eng="sync")
    k.op("sync", lambda e: e.nop(), deps=[k.fence("sync", dma=True)])


def l1f_attn_inputs(xslice_halo, d, l, core_id):
    wqkv = d["L1_ffn_wqkv"][l]
    perm = np.zeros(512, np.int64)
    for h in range(8):
        for dk in range(64):
            i_t = h // 2
            pos = i_t * 128 + (h % 2) * 32 + (dk % 32) + (dk // 32) * 64
            perm[pos] = h * 64 + dk
    wq = wqkv[:, 0:512][:, perm]
    wk = wqkv[:, 512:1024][:, perm]
    qs = d["L1_ffn_q_scale"][l]
    ks = d["L1_ffn_k_scale"][l]
    dk_of_row = (np.arange(128) // 64) * 32 + (np.arange(128) % 32)
    qs_r = qs[dk_of_row]
    ks_r = ks[dk_of_row]
    cols = np.zeros((128, 16), np.float32)
    half_idx = np.arange(0, 64, 2, dtype=np.float64)
    inv_freq = 1.0 / 10000.0 ** (half_idx / 64.0)
    base = (half_idx + 0.4 * 64) / (1.4 * 64)
    cols[:, 0] = inv_freq[np.arange(128) % 32].astype(np.float32)
    cols[:, 1] = np.log(base)[np.arange(128) % 32].astype(np.float32)
    for j in range(4):
        cols[:, 2 + j] = qs_r
        cols[:, 6 + j] = ks_r
    head_of = lambda j, p: 2 * j + ((p // 32) % 2)
    bo = np.zeros((128, 32), np.float32)
    selh = np.zeros((8, 512), np.float32)
    for j in range(4):
        for p in range(128):
            h = head_of(j, p)
            bo[p, j * 8 + h] = 1.0
            selh[h, j * 128 + p] = 1.0
    tglob = core_id * TCORE + np.arange(-512, 2048)
    tpos = np.maximum(tglob, 0).astype(np.float64)
    fr = tpos[None, :] * inv_freq[np.arange(128) % 32][:, None]
    ktab = np.round(fr / (2 * np.pi))
    k2tab = np.round((fr + np.pi / 2) / (2 * np.pi))
    trow = tpos[None, :].astype(np.float32)
    NEGB = -1e9
    i2 = np.arange(512)
    j2 = np.arange(1024)
    ok = (j2[None, :] >= i2[:, None]) & (j2[None, :] <= i2[:, None] + 512)
    mA = np.where(ok, 0.0, NEGB).astype(np.float32)
    mA0 = mA.copy()
    if core_id == 0:
        mA0[:, 0:512] = NEGB
    mAr = mA.reshape(4, 128, 1024).transpose(1, 0, 2).reshape(128, 4096)
    mA0r = mA0.reshape(4, 128, 1024).transpose(1, 0, 2).reshape(128, 4096)
    b = bf16()
    return {
        "x": np.ascontiguousarray(xslice_halo),
        "wq_part": np.ascontiguousarray(wq),
        "wk_part": np.ascontiguousarray(wk),
        "wv_part": np.ascontiguousarray(wqkv[:, 1024:1536]),
        "wo_l": np.ascontiguousarray(d["L1_ffn_wo"][l]),
        "t_row": trow,
        "ktab": ktab.astype(np.float32),
        "k2tab": k2tab.astype(np.float32),
        "maskA": mAr.astype(b),
        "maskA0": mA0r.astype(b),
        "l1fcols": cols,
        "blockones": bo.astype(b),
        "selh": selh.astype(b),
        **const_inputs(),
    }


def build_l1f_mlp(k):
    T, Tc = TCORE, 512
    HP = 2816
    cs = Consts(k)
    pool = PsumPool(k, 4, "pp")
    stage = StagePool(k, "stg", 22, F32)
    x_h = k.dram_in("x", [DM, T + 1], F32)
    xr_h = k.dram_in("xres", [DM, T], F32)
    y_h = k.dram_out("y", [DM, T], F32)
    bflag = k.dram_in("bflag", [1, 1], F32)
    w1g = StreamW(k, "w1g", k.dram_in("w1_g", [DM, HP], F32), DM, HP, stage)
    w1a = StreamW(k, "w1a", k.dram_in("w1_a", [DM, HP], F32), DM, HP, stage)
    w2s = StreamW(k, "w2s", k.dram_in("w2_l", [HP, DM], F32), HP, DM, stage)
    cols_h = k.dram_in("l1mcols", [128, 8 + 8 + 22 + 22 + 8], F32)
    colsb = k.sbuf("l1mcolsb", [128, 68], F32)
    tcols = k.dma("sync", colsb[:], cols_h)
    O_LG, O_LB, O_B1A, O_B1G, O_B2 = 0, 8, 16, 38, 60

    def col(o, i=0):
        return colsb[:, o + i:o + i + 1]

    BOUND = _L1F_BOUNDARY[0]
    xcl = FM(k, "xcl", DM, Tc + 1, F32)
    hn = FM(k, "hn", DM, Tc, F32)
    hnb = FM(k, "hnb", DM, Tc, BF16)
    hb = FM(k, "hb", HP, Tc, BF16)
    s_c = FM(k, "scx", DM, Tc, F32)
    xrc = FM(k, "xrc", DM, Tc, F32) if BOUND else None
    rb = hnb
    stats = k.sbuf("st", [1, 3 * Tc], F32)
    bias2 = k.sbuf("bi", [128, 2 * Tc], F32)

    class _Shift:
        nt, P, T = 8, DM, Tc

        def t(self, j, c0=0, c1=None, r0=0, r1=None):
            off = 1 if j < 4 else 0
            c1 = Tc if c1 is None else c1
            return xcl.t(j, c0 + off, c1 + off, r0=r0, r1=r1)
    shift = _Shift()

    guard = ()
    guard2 = ()
    st_tok = None
    for ci, c0 in enumerate(range(0, T, Tc)):
        t_xl = load_fm(k, xcl, x_h, src=(c0, c0 + Tc + 1),
                       deps=[k.fence("tensor"), st_tok])
        sh_toks = [t_xl] * 8
        tok_n, _, guard = ln_chunk(k, pool, cs, r_fm=shift, sq_bf=hnb, rb_bf=hnb,
                                   y_fm=hn, yb_fm=hnb, stats_sb=stats,
                                   bias_sb=bias2, r_toks=sh_toks, n=Tc,
                                   guard=guard,
                                   gcol=lambda j: col(O_LG, j),
                                   bcol=lambda j: col(O_LB, j))
        t_g = matmul_fm(k, pool, w=w1g, nK=8, nF=22, T=Tc,
                        rhs=lambda kc, a0, a1: hnb.t(kc, a0, a1),
                        evac=lambda e, ps, fo, a0, a1: e.activation(
                            out=hb.t(fo, a0, a1), in_=ps, func=AF.Gelu,
                            bias=col(O_B1G, fo)),
                        evac_eng="scalar", mm_deps=[tok_n, tcols])
        t_a = matmul_fm(k, pool, w=w1a, nK=8, nF=22, T=Tc,
                        rhs=lambda kc, a0, a1: hnb.t(kc, a0, a1),
                        evac=lambda e, ps, fo, a0, a1: e.scalar_tensor_tensor(
                            hb.t(fo, a0, a1), ps, col(O_B1A, fo), hb.t(fo, a0, a1),
                            op0=OP.add, op1=OP.mult),
                        evac_eng="vector", mm_deps=[tok_n, t_g])
        t_2 = matmul_fm(k, pool, w=w2s, nK=22, nF=8, T=Tc,
                        rhs=lambda kc, a0, a1: hb.t(kc, a0, a1),
                        evac=lambda e, ps, fo, a0, a1: e.tensor_scalar_add(
                            s_c.t(fo, a0, a1), ps, col(O_B2, fo)),
                        evac_eng="vector",
                        mm_deps=[t_a] + ([st_tok] if st_tok else []))
        r_toks = [k.op("vector", lambda e, j=j: e.tensor_add(
            s_c.t(j), xcl.t(j, 1, Tc + 1), s_c.t(j)),
            deps=[t_2] if j == 0 else (), inc=True) for j in range(8)]
        if BOUND:
            t_xr = load_fm(k, xrc, xr_h, src=(c0, c0 + Tc), deps=[r_toks[-1]])
            r_toks = [k.op("vector", lambda e, j=j: e.tensor_add(
                s_c.t(j), xrc.t(j), s_c.t(j)),
                deps=[t_xr] if j == 0 else (), inc=True) for j in range(8)]
            tok, _, guard2 = ln_chunk(k, pool, cs, r_fm=s_c, sq_bf=hnb, rb_bf=hnb,
                                      y_fm=s_c, yb_fm=None, stats_sb=stats,
                                      bias_sb=bias2, r_toks=r_toks, n=Tc,
                                      guard=guard2)
            st_tok = store_fm(k, s_c, y_h, dst=(c0, c0 + Tc), deps=[tok],
                              eng="sync")
        else:
            st_tok = store_fm(k, s_c, y_h, dst=(c0, c0 + Tc), deps=[r_toks[-1]],
                              eng="sync")
    k.op("sync", lambda e: e.nop(), deps=[k.fence("sync", dma=True)])


_L1F_BOUNDARY = [False]


def build_l1f_mlp_mid(k):
    _L1F_BOUNDARY[0] = False
    build_l1f_mlp(k)


def build_l1f_mlp_bound(k):
    _L1F_BOUNDARY[0] = True
    build_l1f_mlp(k)


def l1f_mlp_inputs(xmid_halo, d, l, xres=None):
    HID = 2730
    w1 = d["L1_ffn_w1"][l]
    w2 = d["L1_ffn_w2"][l]
    b1 = d["L1_ffn_b1"][l]
    w1a = np.zeros((1024, 2816), np.float32)
    w1g = np.zeros((1024, 2816), np.float32)
    w1a[:, 0:HID] = w1[:, 0:HID]
    w1g[:, 0:HID] = w1[:, HID:2 * HID]
    w2p = np.zeros((2816, 1024), np.float32)
    w2p[0:HID] = w2
    cols = np.zeros((128, 68), np.float32)
    cols[:, 0:8] = colify(d["L1_ffn_ln_g"][l])
    cols[:, 8:16] = colify(d["L1_ffn_ln_b"][l])
    b1a = np.zeros(2816, np.float32)
    b1g = np.zeros(2816, np.float32)
    b1a[0:HID] = b1[0:HID]
    b1g[0:HID] = b1[HID:2 * HID]
    cols[:, 16:38] = colify(b1a)
    cols[:, 38:60] = colify(b1g)
    cols[:, 60:68] = colify(d["L1_ffn_b2"][l])
    m = {
        "x": np.ascontiguousarray(xmid_halo),
        "xres": np.ascontiguousarray(xres if xres is not None else
                                     np.zeros((DM, TCORE), np.float32)),
        "bflag": np.ones((1, 1), np.float32),
        "w1_g": w1g,
        "w1_a": w1a,
        "w2_l": w2p,
        "l1mcols": cols,
        **const_inputs(),
    }
    return m


def _fm_halo(xf, r, n):
    TC = TCORE
    lo = r * TC - n
    if lo < 0:
        return np.ascontiguousarray(np.concatenate(
            [np.zeros((DM, n), np.float32), xf[:, 0:(r + 1) * TC]], axis=1))
    return np.ascontiguousarray(xf[:, lo:(r + 1) * TC])


def _fm_slices(xf):
    TC = TCORE
    return [np.ascontiguousarray(xf[:, r * TC:(r + 1) * TC]) for r in range(8)]


def kernel(**inputs):
    d = {kk: np.asarray(v) for kk, v in inputs.items()}
    xf = np.ascontiguousarray(np.asarray(d["L0_attn_seq_embeddings"],
                                         np.float32)[0].T)
    R = range(8)

    xh = [_fm_halo(xf, r, 3) for r in R]
    res = run_launch(build_l0a, [l0a_inputs(xh[r], d) for r in R])
    in_b = []
    for r in R:
        pcd = np.zeros((128, 16 * 7), np.float32)
        phA = np.zeros((128, 256 * 7), np.float32)
        for i in range(r):
            s = 7 - r + i
            pcd[:, s * 16:(s + 1) * 16] = res[i]["cdsum"]
            phA[:, s * 256:(s + 1) * 256] = res[i]["hA"]
        in_b.append(l0b_inputs(xh[r], d, pcd, phA))
    res_b = run_launch(build_l0b, in_b)
    allp = np.zeros((128, 16), np.float32)
    for r in R:
        allp[:, 2 * r:2 * r + 2] = res_b[r]["part"]
    xs = _fm_slices(xf)
    res_c = run_launch(build_l0c,
                       [l0c_inputs(xs[r], res_b[r]["c"], allp) for r in R])
    xf = np.concatenate([res_c[r]["y"] for r in R], axis=1)

    xs = _fm_slices(xf)
    res = run_launch(build_l0f, [l0f_inputs(xs[r], d) for r in R])
    xf = np.concatenate([res[r]["y"] for r in R], axis=1)

    xs = _fm_slices(xf)
    res0 = run_launch(build_l1a0, [l1a0_inputs(xs[r], d) for r in R])
    pss, ppp = [], []
    for r in R:
        a = np.zeros((128, 8 * 7), np.float32)
        b = np.tile(np.ones((128, 8), np.float32), (1, 7))
        for i in range(r):
            s = 7 - r + i
            a[:, s * 8:(s + 1) * 8] = res0[i]["ssum"]
            b[:, s * 8:(s + 1) * 8] = res0[i]["parprod"]
        pss.append(a)
        ppp.append(b)
    res_a = run_launch(build_l1a_a,
                       [l1a_inputs(xs[r], d, False, pss[r], ppp[r]) for r in R])
    in_b = []
    for r in R:
        pS = np.zeros((128, 2048 * 7), np.float32)
        pg = np.zeros((128, 8 * 7), np.float32)
        for i in range(r):
            s = 7 - r + i
            pS[:, s * 2048:(s + 1) * 2048] = res_a[i]["S"]
            pg[:, s * 8:(s + 1) * 8] = res_a[i]["gsum"]
        in_b.append(l1a_inputs(xs[r], d, True, pss[r], ppp[r], pS, pg))
    res_b = run_launch(build_l1a_b, in_b)
    xf = np.concatenate([res_b[r]["y"] for r in R], axis=1)

    x3f = xf
    for l in range(2):
        res = run_launch(build_l1f_attn,
                         [l1f_attn_inputs(_fm_halo(xf, r, 513), d, l, r)
                          for r in R])
        xmf = np.concatenate([res[r]["y"] for r in R], axis=1)
        if l == 0:
            res = run_launch(build_l1f_mlp_mid,
                             [l1f_mlp_inputs(_fm_halo(xmf, r, 1), d, l)
                              for r in R])
        else:
            x3s = _fm_slices(x3f)
            res = run_launch(build_l1f_mlp_bound,
                             [l1f_mlp_inputs(_fm_halo(xmf, r, 1), d, l, x3s[r])
                              for r in R])
        xf = np.concatenate([res[r]["y"] for r in R], axis=1)

    xs = _fm_slices(xf)
    res_a = run_launch(build_l2a_a, [l2a_a_inputs(xs[r]) for r in R])
    pall = np.concatenate([res_a[r]["pooled"] for r in R], axis=1)
    res_b = run_launch(build_l2a_b,
                       [l2a_b_inputs(_fm_halo(xf, r, 128), d, pall, r)
                        for r in R])
    xf = np.concatenate([res_b[r]["y"] for r in R], axis=1)

    xs = _fm_slices(xf)
    res = run_launch(build_l2f, [l2f_inputs(xs[r], d) for r in R])
    xf = np.concatenate([res[r]["y"] for r in R], axis=1)

    xs = _fm_slices(xf)
    res_a = run_launch(build_l3a_a, [l3a_inputs(xs[r], d, False) for r in R])
    in_b = []
    for r in R:
        pS = np.zeros((128, 1024 * 7), np.float32)
        pg = np.zeros((128, 8 * 7), np.float32)
        for i in range(r):
            s = 7 - r + i
            pS[:, s * 1024:(s + 1) * 1024] = res_a[i]["S"]
            pg[:, s * 8:(s + 1) * 8] = res_a[i]["gsum"]
        in_b.append(l3a_inputs(xs[r], d, True, pS, pg))
    res_b = run_launch(build_l3a_b, in_b)
    xf = np.concatenate([res_b[r]["y"] for r in R], axis=1)

    xs = _fm_slices(xf)
    res = run_launch(build_l3f, [l3f_inputs(xs[r], d) for r in R])
    xf = np.concatenate([res[r]["y"] for r in R], axis=1)

    return np.ascontiguousarray(xf.T)[None].astype(np.float32)

